# version 28
# baseline (speedup 1.0000x reference)
import contextlib
import math
import numpy as np
import ml_dtypes

import concourse.bass as bass
import concourse.dve_ops as dve_ops
import concourse.mybir as mybir
from concourse.bass_utils import run_bass_kernel_spmd

F32 = mybir.dt.float32
BF16 = mybir.dt.bfloat16
AF = mybir.ActivationFunctionType
AL = mybir.AluOpType
AX = mybir.AxisListType

D = 1024
H = 16
KVH = 4
HD = 64
LAT = 256
WIN = 256
NE = 16
TOPK = 4
FF = 2048
NCH = D // 128
BASE_G, BASE_L = 1.0e6, 1.0e4
EPS_LN = 1e-5
EPS_RMS = 1e-6

SLOT_A = [0, 1, 2, 3, 8, 9, 10, 11]
SLOT_B = [4, 5, 6, 7, 12, 13, 14, 15]
EO = np.concatenate([np.arange(0, HD, 2), np.arange(1, HD, 2)])


def _perm_q():
    cols = []
    for j in range(8):
        cols.append(64 * SLOT_A[j] + EO)
        cols.append(64 * SLOT_B[j] + EO)
    return np.concatenate(cols)


def _perm_k():
    return np.concatenate([64 * kh + EO for kh in range(KVH)])


def _perm_o():
    rows = []
    for j in range(8):
        rows.append(64 * SLOT_A[j] + np.arange(64))
        rows.append(64 * SLOT_B[j] + np.arange(64))
    return np.concatenate(rows)


def _rope_tables(S, base):
    i = np.arange(32, dtype=np.float32)
    invf = (1.0 / (base ** (2.0 * i / HD))).astype(np.float32)
    t = np.arange(S, dtype=np.float32)
    ang = t[None, :] * invf[:, None]
    cos = np.cos(ang).astype(np.float32)
    sin = np.sin(ang).astype(np.float32)
    cos128 = np.tile(cos, (4, 1))
    sin128 = np.tile(sin, (4, 1))
    return np.ascontiguousarray(cos128), np.ascontiguousarray(sin128)


def _tri_masks():
    p = np.arange(128)[:, None]
    q = np.arange(128)[None, :]
    tri_l = np.where(p > q, np.float32(-1e30), np.float32(0.0)).astype(np.float32)
    tri_u = np.where(p <= q, np.float32(-1e30), np.float32(0.0)).astype(np.float32)
    return tri_l, tri_u



class _Step:
    __slots__ = ("fn", "waits", "incs", "drain", "seq", "note", "_dma_val", "is_dma", "pos")

    def __init__(self, fn, seq, note):
        self.fn = fn
        self.waits = []
        self.incs = []
        self.drain = False
        self.seq = seq
        self.note = note
        self.is_dma = False
        self.pos = -1


class EM:
    ENGINES = ("tensor", "vector", "scalar", "gpsimd", "sync")

    def __init__(self, nc):
        self.nc = nc
        self.steps = {e: [] for e in self.ENGINES}
        self.seq = 0
        self.sems = {}
        self.sem_handles = {}
        for e in self.ENGINES:
            self.sems["tk_" + e] = 0
        self.incs = {e: [] for e in self.ENGINES}
        self.last_step = {e: None for e in self.ENGINES}
        self.last_drain = {e: -1 for e in self.ENGINES}
        self.waited = {e: {} for e in self.ENGINES}
        self.last_w = {}
        self.readers = {}
        self.alias = {}

    def new_dma_sem(self, name):
        if name not in self.sems:
            self.sems[name] = 0

    def _engine_token(self, eng, step):
        for (sq, val, st) in self.incs[eng]:
            if sq >= step.seq:
                assert st.pos >= step.pos, (
                    f"token attach bug: need pos {step.pos} ({step.note}) got "
                    f"pos {st.pos} ({st.note}) sq={sq} vs {step.seq}")
                return ("tk_" + eng, val)
        last = self.last_step[eng]
        assert last is not None
        if last.is_dma or last.fn is None or getattr(last, "note", "") in ("barrier", "cpw", "tok_drain"):
            if eng == "sync":
                last = _Step(lambda e: e.nop(), self.seq, "tok_drain")
            else:
                last = _Step(lambda e: e.drain(), self.seq, "tok_drain")
            self.seq += 1
            last.pos = len(self.steps[eng])
            self.steps[eng].append(last)
            self.last_step[eng] = last
            self.last_drain[eng] = last.seq
        self.sems["tk_" + eng] += 1
        val = self.sems["tk_" + eng]
        last.incs.append(("tk_" + eng, 1))
        self.incs[eng].append((last.seq, val, last))
        return ("tk_" + eng, val)

    def _token_of(self, prod):
        kind, step, eng = prod
        if kind == "eng":
            return self._engine_token(eng, step)
        else:
            return (kind, step._dma_val)

    def op(self, eng, fn, reads=(), writes=(), note="", extra_waits=()):
        step = _Step(fn, self.seq, note)
        self.seq += 1
        waits = list(extra_waits)
        no_drain = (eng == "tensor")
        rd = [self.alias.get(r if isinstance(r, str) else r.name,
                             r if isinstance(r, str) else r.name) for r in reads]
        wr = [self.alias.get(w if isinstance(w, str) else w.name,
                             w if isinstance(w, str) else w.name) for w in writes]
        for r in rd:
            p = self.last_w.get(r)
            if p is None:
                continue
            if p[0] == "eng" and p[2] == eng:
                if p[1].seq >= self.last_drain[eng] and not no_drain:
                    step.drain = True
            else:
                waits.append(self._token_of(p))
        for w in wr:
            p = self.last_w.get(w)
            if p is not None:
                if p[0] == "eng" and p[2] == eng:
                    if p[1].seq >= self.last_drain[eng] and not no_drain:
                        step.drain = True
                else:
                    waits.append(self._token_of(p))
            for rp in self.readers.get(w, ()):
                if rp[0] == "eng" and rp[2] == eng:
                    if rp[1].seq >= self.last_drain[eng] and not no_drain:
                        step.drain = True
                    continue
                waits.append(self._token_of(rp))
        wt = self.waited[eng]
        merged = {}
        for (s, v) in waits:
            if v > wt.get(s, 0) and v > merged.get(s, 0):
                merged[s] = v
        for s, v in merged.items():
            step.waits.append((s, v))
            wt[s] = v
        if step.drain:
            self.last_drain[eng] = step.seq
        step.pos = len(self.steps[eng])
        self.steps[eng].append(step)
        self.last_step[eng] = step
        me = ("eng", step, eng)
        for w in wr:
            self.last_w[w] = me
            self.readers[w] = []
        for r in rd:
            self.readers.setdefault(r, []).append(me)
        return step

    def dma(self, eng, out_ap, in_ap, sem, reads=(), writes=(), note=""):
        self.new_dma_sem(sem)
        step = self.op(eng, None, reads=reads, writes=writes, note=note)
        step.fn = lambda e, o=out_ap, i=in_ap: e.dma_start(out=o, in_=i)
        self.sems[sem] += 16
        val = self.sems[sem]
        step.incs.append((sem, 16))
        step._dma_val = val
        step.is_dma = True
        me = (sem, step, eng)
        wr = [self.alias.get(w if isinstance(w, str) else w.name,
                             w if isinstance(w, str) else w.name) for w in writes]
        rd = [self.alias.get(r if isinstance(r, str) else r.name,
                             r if isinstance(r, str) else r.name) for r in reads]
        for w in wr:
            self.last_w[w] = me
            self.readers[w] = []
        for r in rd:
            rl = self.readers.setdefault(r, [])
            if rl and rl[-1][0] == "eng" and rl[-1][1] is step:
                rl[-1] = me
        return (sem, val)

    def barrier(self):
        toks = {}
        for e in self.ENGINES:
            if self.last_step[e] is not None:
                toks[e] = self._engine_token(e, self.last_step[e])
        dma_toks = [(s, v) for s, v in self.sems.items()
                    if not s.startswith("tk_") and v > 0]
        for e in self.ENGINES:
            step = _Step(lambda eng: eng.nop(), self.seq, "barrier")
            self.seq += 1
            step.pos = len(self.steps[e])
            if e != "sync":
                step.drain = True
                self.last_drain[e] = step.seq
            wt = self.waited[e]
            for oe, (s, v) in toks.items():
                if oe == e:
                    continue
                if v > wt.get(s, 0):
                    step.waits.append((s, v))
                    wt[s] = v
            for (s, v) in dma_toks:
                if v > wt.get(s, 0):
                    step.waits.append((s, v))
                    wt[s] = v
            self.steps[e].append(step)
            self.last_step[e] = step

    def replay(self, block):
        def body(engname):
            def run(engine):
                for st in self.steps[engname]:
                    try:
                        for (s, v) in st.waits:
                            engine.wait_ge(self.sem_handles[s], v)
                        if st.drain and engname != "sync":
                            engine.drain()
                        inst = st.fn(engine)
                        if st.note:
                            inst.annotate(st.note)
                        for (s, a) in st.incs:
                            inst.then_inc(self.sem_handles[s], a)
                    except Exception:
                        import sys as _s
                        print(f"REPLAY FAIL eng={engname} note={st.note}", file=_s.stderr)
                        raise
            return run

        block.tensor(body("tensor"))
        block.vector(body("vector"))
        block.scalar(body("scalar"))
        block.gpsimd(body("gpsimd"))
        block.sync(body("sync"))



def build_program(S=2048, dbg=()):
    NT = S // 512
    NKC = S // 128
    NHALF = max(1, S // 1024)
    HTOK = S // NHALF

    nc = bass.Bass("TRN2", target_bir_lowering=False)
    es = contextlib.ExitStack()

    def din(name, shape, dt=F32):
        return nc.dram_tensor(name, shape, dt, kind="ExternalInput").ap()

    x_in = din("x_fm", [D, S])
    wq_g_in = din("g_wq_p", [D, D])
    kvd_in = din("g_kv_down", [D, LAT])
    kup_in = din("g_k_up_p", [LAT, LAT])
    vup_in = din("g_v_up", [LAT, LAT])
    wo_g_in = din("g_wo_p", [D, D])
    wq_l_in = din("l_wq_p", [D, D])
    wk_l_in = din("l_wk_p", [D, LAT])
    wv_l_in = din("l_wv", [D, LAT])
    wo_l_in = din("l_wo_p", [D, D])
    n1w_in = din("n1w", [128, NCH])
    n2w_in = din("n2w", [128, NCH])
    rw_in = din("router_w", [D, NE])
    rb_in = din("router_b", [1, NE])
    w1_in = din("w1", [NE, D, FF])
    b1_in = din("b1r", [128, NE * (FF // 128)])
    w2_in = din("w2", [NE, FF, D])
    b2_in = din("b2_all", [NE, D])
    cosg_in = din("cos_g", [128, S])
    sing_in = din("sin_g", [128, S])
    cosl_in = din("cos_l", [128, S])
    sinl_in = din("sin_l", [128, S])
    tril_in = din("tri_l", [128, 128])
    triu_in = din("tri_u", [128, 128])
    maskg_in = din("mask_g", [4, 128, 512], BF16)
    maskl_in = din("mask_l", [6, 128, 512], BF16)
    idf_in = din("ident_f32", [128, 128])
    idb_in = din("ident_bf16", [128, 128], BF16)
    ones128_in = din("ones_1x128", [1, 128])
    onescol_in = din("ones_c128", [128, 1])
    cpool_in = din("cpool", [128, 8])
    onesall_in = din("ones_all", [128, 128])
    pswap_in = din("p_swap", [128, 128])
    ones64_in = din("ones_1x64", [1, 64])
    blk2_in = din("blk2", [128, 2])
    blk2b_in = din("blk2b", [128, 2], BF16)
    ind2_in = din("ind2", [2, 128])
    ind2b_in = din("ind2b", [2, 128], BF16)

    out_t = nc.dram_tensor("out", [D, S], F32, kind="ExternalOutput").ap()

    y_sp = nc.dram_tensor("y_sp", [NT, NCH, 128, 512], F32, kind="Internal").ap()
    y2_sp = nc.dram_tensor("y2_sp", [NT, NCH, 128, 512], F32, kind="Internal").ap()
    q_sp = nc.dram_tensor("q_sp", [NT, NCH, 128, 512], BF16, kind="Internal").ap()
    ql_sp = nc.dram_tensor("ql_sp", [NT, NCH, 128, 512], BF16, kind="Internal").ap()
    z_sp = nc.dram_tensor("z_sp", [NT, NCH, 128, 512], BF16, kind="Internal").ap()
    xn_sp = nc.dram_tensor("xn_sp", [NT, NCH, 128, 512], F32, kind="Internal").ap()

    dbg_outs = {}
    def dbg_tensor(name, shape, dt=F32):
        ap = nc.dram_tensor("dbg_" + name, shape, dt, kind="ExternalOutput").ap()
        dbg_outs[name] = ap
        return ap

    em = EM(nc)

    def _sb(arena, name, shape, dt):
        t = arena.sbuf_tensor(name, shape, dt)
        em.alias[t.name] = name
        return t

    perm = nc.auto_arena()
    ones128 = _sb(perm, "ones128", [1, 128], F32)
    onescol = _sb(perm, "onescol", [128, 1], F32)
    cpool = _sb(perm, "cpool", [128, 8], F32)
    onesall = _sb(perm, "onesall", [128, 128], F32)
    pswap = _sb(perm, "pswap", [128, 128], F32)
    ones64 = _sb(perm, "ones64", [1, 64], F32)
    blk2 = _sb(perm, "blk2", [128, 2], F32)
    blk2b = _sb(perm, "blk2b", [128, 2], BF16)
    ind2 = _sb(perm, "ind2", [2, 128], F32)
    ind2b = _sb(perm, "ind2b", [2, 128], BF16)
    n1w = _sb(perm, "n1w", [128, NCH], F32)
    n2w = _sb(perm, "n2w", [128, NCH], F32)
    rwb = _sb(perm, "rwb", [128, NCH * NE], BF16)
    rbb = _sb(perm, "rbb", [1, NE], F32)
    b1b = _sb(perm, "b1b", [128, NE * (FF // 128)], F32)
    b2b = _sb(perm, "b2b", [16, D], F32)
    cfm = _sb(perm, "cfm", [16, S], F32)
    kg = _sb(perm, "kg", [128, 2 * S], BF16)
    vg = _sb(perm, "vg", [128, NKC * 260], BF16)
    kl = _sb(perm, "kl", [128, 2 * S], BF16)
    vl = _sb(perm, "vl", [128, NKC * 260], BF16)
    perm.allocate()

    PS = [es.enter_context(nc.psum_tensor(f"P{i}", [128, 512], F32)) for i in range(7)]
    PT = es.enter_context(nc.psum_tensor("PT", [128, 512], BF16))

    def mm(lhsT, rhs, out, start, stop, reads, bank, tile_position=None, note=""):
        def f(e, l=lhsT, r=rhs, o=out, s1=start, s2=stop, tp=tile_position):
            return e.matmul(o, l, r, start=s1, stop=s2, tile_position=tp)
        return em.op("tensor", f, reads=reads, writes=[bank], note=note)

    def v_op(fn, reads, writes, note=""):
        return em.op("vector", fn, reads=reads, writes=writes, note=note)

    def s_op(fn, reads, writes, note=""):
        return em.op("scalar", fn, reads=reads, writes=writes, note=note)

    def g_op(fn, reads, writes, note=""):
        return em.op("gpsimd", fn, reads=reads, writes=writes, note=note)

    def recip_any(dst, src, scratch, src_names, dst_name, scratch_name, tag=""):
        s_op(lambda e, d=scratch, s=src: e.activation(d, s, AF.Ln),
             reads=src_names, writes=[scratch_name], note="rl" + tag)
        s_op(lambda e, d=dst, s=scratch: e.activation(d, s, AF.Exp, scale=-1.0),
             reads=[scratch_name], writes=[dst_name], note="re" + tag)

    def rsqrt2(dst, src, src_names, dst_name, tag=""):
        s_op(lambda e, d=dst, s=src: e.activation(d, s, AF.Ln),
             reads=src_names, writes=[dst_name], note="ql" + tag)
        s_op(lambda e, d=dst: e.activation(d, d, AF.Exp, scale=-0.5),
             reads=[dst_name], writes=[dst_name], note="qe" + tag)

    def rsqrt_any(dst, src, scratch, src_names, dst_name, scratch_name, tag=""):
        s_op(lambda e, d=scratch, s=src: e.activation(d, s, AF.Ln),
             reads=src_names, writes=[scratch_name], note="ql" + tag)
        s_op(lambda e, d=dst, s=scratch: e.activation(d, s, AF.Exp, scale=-0.5),
             reads=[scratch_name], writes=[dst_name], note="qe" + tag)

    def recip_rows(dst, src, n, src_names, dst_name, newton=True):
        t1 = rows[0:1, 512 * 4: 512 * 4 + n]
        recip_any(dst, src, t1, src_names, dst_name, "rows", "r")

    def ld(dst_t, src_ap, sem, note, eng="sync"):
        return em.dma(eng, dst_t[:], src_ap, "s_" + note, reads=[],
                      writes=[dst_t.name], note=note)

    cp_tok = ld(cpool, cpool_in[:], "c_ld", "cpool")
    nc.const_aps.register(F32, 0.125, cpool[:, 4:5])
    nc.const_aps.register(F32, -1.0, cpool[:, 5:6])
    nc.const_aps.register(F32, -0.5, cpool[:, 6:7])
    for _e in EM.ENGINES:
        em.op(_e, lambda en: en.nop(), reads=[], writes=[], note="cpw",
              extra_waits=[cp_tok])
    ld(ones128, ones128_in[:], "c_ld", "ones128")
    ld(onescol, onescol_in[:], "c_ld", "onescol")
    ld(onesall, onesall_in[:], "c_ld", "onesall")
    ld(pswap, pswap_in[:], "c_ld", "pswap")
    ld(ones64, ones64_in[:], "c_ld", "ones64")
    ld(blk2, blk2_in[:], "c_ld", "blk2")
    ld(blk2b, blk2b_in[:], "c_ld", "blk2b")
    ld(ind2, ind2_in[:], "c_ld", "ind2")
    ld(ind2b, ind2b_in[:], "c_ld", "ind2b")
    ld(n1w, n1w_in[:], "c_ld", "n1w")
    ld(n2w, n2w_in[:], "c_ld", "n2w")
    em.dma("gpsimd", rwb[:].rearrange("p (c n) -> p c n", c=NCH),
           rw_in.rearrange("(c p) n -> p c n", p=128), "s_rw",
           reads=[], writes=["rwb"], note="rw")
    ld(rbb, rb_in[:], "c_ld", "rb")
    ld(b1b, b1_in[:], "c_ld", "b1")
    ld(b2b, b2_in[:], "c_ld", "b2")

    def wload(dst_t, src_ap, din_, dout, sem, cols=None, note="w"):
        nch = din_ // 128
        dst = dst_t[:] if cols is None else dst_t[:, cols[0]:cols[1]]
        return em.dma("gpsimd",
                      dst.rearrange("p (c n) -> p c n", c=nch),
                      src_ap.rearrange("(c p) n -> p c n", p=128),
                      sem if sem.startswith("wf") else "s_" + note,
                      reads=[], writes=[dst_t.name], note=note)

    pha = nc.auto_arena()
    wqg = _sb(pha, "wqg", [128, NCH * D], BF16)
    kvd = _sb(pha, "kvd", [128, NCH * LAT], BF16)
    kup = _sb(pha, "kup", [128, 2 * LAT], BF16)
    vup = _sb(pha, "vup", [128, 2 * LAT], BF16)
    cosg = _sb(pha, "cosg", [128, S], F32)
    sing = _sb(pha, "sing", [128, S], F32)
    rows = _sb(pha, "rowsA", [1, 2560], F32)
    rowsbP = [_sb(pha, f"rowsbA{i}", [2, 1536], F32) for i in range(2)]
    rowsb = rowsbP[0]
    idb = _sb(pha, "idbA", [128, 128], BF16)
    em.alias[rows.name] = "rows"
    em.alias[rowsbP[0].name] = "rowsb0"
    em.alias[rowsbP[1].name] = "rowsb1"
    em.alias[idb.name] = "idb"
    xta = [_sb(pha, f"xt{i}", [128, NCH * 512], F32) for i in range(2)]
    yta = _sb(pha, "yta", [128, NCH * 512], F32)
    t1a = _sb(pha, "t1a", [128, NCH * 512], BF16)
    sqa = [_sb(pha, f"sqa{i}", [128, 512], F32) for i in range(2)]
    pcav = [_sb(pha, f"pca{i}", [128, 512], F32) for i in range(2)]
    psav = [_sb(pha, f"psa{i}", [128, 512], F32) for i in range(2)]
    qsqv = [_sb(pha, f"qsq{i}", [128, 512], BF16) for i in range(2)]
    rbt2 = [_sb(pha, f"rbt2A{i}", [2, 1024], BF16) for i in range(2)]
    em.alias[rbt2[0].name] = "rowsb2_0"
    em.alias[rbt2[1].name] = "rowsb2_1"
    pca, psa = pcav[0], psav[0]
    qba = [_sb(pha, f"qba{i}", [128, 512], BF16) for i in range(2)]
    lata = _sb(pha, "lata", [128, 2 * 512], BF16)
    vtmp = _sb(pha, "vtmp", [128, 512], BF16)
    pha.allocate()

    wload(wqg, wq_g_in, D, D, "wA", note="wqg")
    wload(kvd, kvd_in, D, LAT, "wA", note="kvd")
    wload(kup, kup_in, LAT, LAT, "wA", note="kup")
    wload(vup, vup_in, LAT, LAT, "wA", note="vup")
    ld(cosg, cosg_in[:], "c_ld", "cosg")
    ld(sing, sing_in[:], "c_ld", "sing")
    ld(idb, idb_in[:], "c_ld", "idbA")

    v_op(lambda e: e.memset(vg[:], 1.0), reads=[], writes=["vg"], note="vg1")
    v_op(lambda e: e.memset(vl[:], 1.0), reads=[], writes=["vl"], note="vl1")

    def x_load(t, slot):
        return em.dma("sync",
                      xta[slot][:].rearrange("p (c n) -> p c n", c=NCH),
                      x_in.rearrange("(c p) n -> p c n", p=128)[:, :, 512 * t:512 * t + 512],
                      f"xld{slot}", reads=[], writes=[xta[slot].name], note=f"x{t}")

    x_load(0, 0)


    def row(i, off, n=512):
        return rows[0:1, off:off + n]

    def sum_over_chunks(src_tile, bank, square, tag, nch=NCH, width=512):
        sq_tok = []
        for c in range(nch):
            sl = src_tile[:, width * c: width * (c + 1)]
            if square:
                sb = sqa[c % 2]
                s_op(lambda e, o=sb[:, 0:width], i=sl: e.activation(o, i, AF.Square),
                     reads=[src_tile.name], writes=[sb.name], note=f"sq_{tag}{c}")
                srcap = sb[:, 0:width]
                rname = sb.name
            else:
                srcap = sl
                rname = src_tile.name
            mm(onescol[:], srcap, bank[0:1, :], start=(c == 0), stop=(c == nch - 1),
               reads=["onescol", rname], bank=bank, note=f"sum_{tag}{c}")

    def ln_mu(bank_s, roff, tag):
        mu = row(0, roff + 1024)
        v_op(lambda e, o=mu, i=bank_s[0:1, :]: e.tensor_scalar(o, i, 1.0 / D, None, AL.mult),
             reads=[bank_s.name], writes=["rows"], note=f"mu_{tag}")
        return mu

    def ln_rows(bank_q, eps, roff, tag):
        mu = row(0, roff + 1024)
        m2 = row(0, roff + 1536)
        v_op(lambda e, o=m2, i=mu: e.tensor_mul(o, i, i),
             reads=["rows"], writes=["rows"], note=f"m2_{tag}")
        q2 = row(0, roff + 2048)
        v_op(lambda e, o=q2, i=bank_q[0:1, :]: e.tensor_scalar(o, i, 1.0 / D, None, AL.mult),
             reads=[bank_q.name], writes=["rows"], note=f"q2_{tag}")
        var = row(0, roff + 2048)
        v_op(lambda e, o=var, a=q2, b=m2: e.tensor_sub(o, a, b),
             reads=["rows"], writes=["rows"], note=f"var_{tag}")
        v_op(lambda e, o=var, ep=eps: e.tensor_scalar(o, o, ep, None, AL.add),
             reads=["rows"], writes=["rows"], note=f"vep_{tag}")
        A = row(0, roff)
        rsqrt_any(A, var, row(0, roff + 1536), ["rows"], "rows", "rows", f"L{tag}")
        B = row(0, roff + 512)
        v_op(lambda e, o=B, a=mu, b=A: e.tensor_mul(o, a, b),
             reads=["rows"], writes=["rows"], note=f"nb_{tag}")
        v_op(lambda e, o=B: e.tensor_scalar(o, o, -1.0, None, AL.mult),
             reads=["rows"], writes=["rows"], note=f"nb2_{tag}")
        return A, B

    def rms_row(bank_q, eps, roff, tag):
        m = row(0, roff + 1024)
        v_op(lambda e, o=m, i=bank_q[0:1, :]:
             e.tensor_scalar(o, i, 1.0 / D, eps, AL.mult, AL.add),
             reads=[bank_q.name], writes=["rows"], note=f"rm_{tag}")
        R = row(0, roff)
        rsqrt_any(R, m, row(0, roff + 512), ["rows"], "rows", "rows", f"R{tag}")
        return R

    def bcast(src_row, bank, src_names, note, lhs=None, base=None):
        l = ones128[:] if lhs is None else lhs
        outap = bank[:, :] if base is None else bank[base[0]:base[1], :]
        tp = None if base is None else (0, base[0])
        mm(l, src_row, outap, True, True, reads=["ones128" if lhs is None else "x", *src_names],
           bank=bank, tile_position=tp, note=note)

    def rope_qknorm_start(bank, ctab, stab, ctname, tag, par):
        pc = pcav[par][:]
        ps_ = psav[par][:]
        pcn = "pca" + str(par)
        psn = "psa" + str(par)
        bank_s = PS[0 + par]
        s_op(lambda e, o=ps_, i=bank[:, :]: e.activation(o, i, AF.Identity),
             reads=[bank.name], writes=[psn], note=f"qc_{tag}")
        mm(pswap[:], ps_, bank_s[:, :], True, True,
           reads=["pswap", psn], bank=bank_s, note=f"sw_{tag}")
        v_op(lambda e, o=pc, p=ps_, c=ctab: e.tensor_mul(o, p, c),
             reads=[psn, ctname[0]], writes=[pcn], note=f"pc_{tag}")
        v_op(lambda e, o=ps_, p=bank_s[:, :], s=stab: e.tensor_mul(o, p, s),
             reads=[bank_s.name, ctname[1]], writes=[psn], note=f"ps_{tag}")
        v_op(lambda e, o=pc, b=ps_: e.tensor_add(o, o, b),
             reads=[pcn, psn], writes=[pcn], note=f"rc_{tag}")
        sb = qsqv[par]
        s_op(lambda e, o=sb[:, 0:512], i=pc: e.activation(o, i, AF.Square),
             reads=[pcn], writes=[sb.name], note=f"qn2_{tag}")
        return

    def rope_qknorm_finish(dest_ap, dest_name, tag, par):
        pc = pcav[par][:]
        pcn = "pca" + str(par)
        bank_m = PS[2 + par]
        rbt = rowsbP[par]
        rbn = "rowsb" + str(par)
        sb = qsqv[par]
        mm(blk2b[:], sb[:, 0:512], bank_m[0:2, :], True, True,
           reads=["blk2b", sb.name], bank=bank_m, note=f"qns_{tag}")
        qm = rbt[0:2, 0:512]
        v_op(lambda e, o=qm, i=bank_m[0:2, :]:
             e.tensor_scalar(o, i, 1.0 / HD, EPS_RMS, AL.mult, AL.add),
             reads=[bank_m.name], writes=[rbn], note=f"qnm_{tag}")
        rq = rbt[0:2, 512:1024]
        rsqrt2(rq, qm, [rbn], rbn, f"q{tag}")
        rqh = rbt2[par][0:2, 0:512]
        rql = rbt[0:2, 1024:1536]
        rqlb = rbt2[par][0:2, 512:1024]
        bn2 = "rowsb2_" + str(par)
        v_op(lambda e, o=rqh, i=rq: e.tensor_copy(o, i),
             reads=[rbn], writes=[bn2], note=f"qh_{tag}")
        v_op(lambda e, o=rql, a=rq, b=rqh: e.tensor_sub(o, a, b),
             reads=[rbn, bn2], writes=[rbn], note=f"qlo_{tag}")
        v_op(lambda e, o=rqlb, i=rql: e.tensor_copy(o, i),
             reads=[rbn], writes=[bn2], note=f"qlc_{tag}")
        mm(ind2b[:], rqh, bank_m[:, :], True, False,
           reads=["ind2b", bn2], bank=bank_m, note=f"qnb_{tag}")
        mm(ind2b[:], rqlb, bank_m[:, :], False, True,
           reads=["ind2b", bn2], bank=bank_m, note=f"qnb2_{tag}")
        v_op(lambda e, o=dest_ap, a=pc, b=bank_m[:, :]: e.tensor_mul(o, a, b),
             reads=[pcn, bank_m.name], writes=[dest_name], note=f"qnf_{tag}")

    for t in range(NT):
        xt = xta[t % 2]
        if t + 1 < NT:
            x_load(t + 1, (t + 1) % 2)
        sum_over_chunks(xt, PS[6], False, f"sx{t}")
        ln_mu(PS[6], 0, f"ln1_{t}")
        sum_over_chunks(xt, PS[6], True, f"sxx{t}")
        A, B = ln_rows(PS[6], EPS_LN, 0, f"ln1_{t}")
        bcast(A, PS[2], ["rows"], f"bA{t}")
        bcast(B, PS[3], ["rows"], f"bB{t}")
        for c in range(NCH):
            v_op(lambda e, o=yta[:, 512 * c:512 * (c + 1)], x=xt[:, 512 * c:512 * (c + 1)],
                 a=PS[2][:, :]: e.tensor_mul(o, x, a),
                 reads=[xt.name, "P2"], writes=["yta"], note=f"y1_{t}{c}")
            v_op(lambda e, o=yta[:, 512 * c:512 * (c + 1)], b=PS[3][:, :]: e.tensor_add(o, o, b),
                 reads=["P3", "yta"], writes=["yta"], note=f"y2_{t}{c}")
        em.dma("gpsimd", y_sp[t].rearrange("c p n -> p c n"),
               yta[:].rearrange("p (c n) -> p c n", c=NCH),
               f"ysp{t}", reads=["yta"], writes=["y_sp"], note=f"ysp{t}")
        sum_over_chunks(yta, PS[6], True, f"syy{t}")
        R = rms_row(PS[6], EPS_RMS, 0, f"rms1_{t}")
        bcast(R, PS[2], ["rows"], f"bR{t}")
        for c in range(NCH):
            v_op(lambda e, o=t1a[:, 512 * c:512 * (c + 1)], y=yta[:, 512 * c:512 * (c + 1)],
                 w=n1w[:, c:c + 1], r=PS[2][:, :]:
                 e.scalar_tensor_tensor(o, y, w, r, AL.mult, AL.mult),
                 reads=["yta", "n1w", "P2"], writes=["t1a"], note=f"t1_{t}{c}")
        pend = None
        for oc in range(NCH):
            bq = PS[4 + (oc % 2)]
            for c in range(NCH):
                mm(wqg[:, D * c + 128 * oc: D * c + 128 * (oc + 1)],
                   t1a[:, 512 * c:512 * (c + 1)], bq[:, :],
                   start=(c == 0), stop=(c == NCH - 1),
                   reads=["wqg", "t1a"], bank=bq, note=f"qp{t}_{oc}{c}")
            def _flush_q(p):
                qb_, tg_, par_, oc_ = p
                rope_qknorm_finish(qb_[:, 0:512], qb_.name, tg_, par_)
                em.dma("gpsimd", q_sp[t, oc_], qb_[:, 0:512], f"qsp{t}",
                       reads=[qb_.name], writes=["q_sp"], note=f"qsp{t}_{oc_}")
            if pend is not None:
                _flush_q(pend)
            rope_qknorm_start(bq, cosg[:, 512 * t:512 * (t + 1)],
                              sing[:, 512 * t:512 * (t + 1)], ("cosg", "sing"),
                              f"q{t}_{oc}", oc % 2)
            pend = (qba[oc % 2], f"q{t}_{oc}", oc % 2, oc)
        _flush_q(pend)
        for lc in range(2):
            bq = PS[4]
            for c in range(NCH):
                mm(kvd[:, LAT * c + 128 * lc: LAT * c + 128 * (lc + 1)],
                   t1a[:, 512 * c:512 * (c + 1)], bq[:, :],
                   start=(c == 0), stop=(c == NCH - 1),
                   reads=["kvd", "t1a"], bank=bq, note=f"lat{t}_{lc}{c}")
            s_op(lambda e, o=lata[:, 512 * lc:512 * (lc + 1)], i=bq[:, :]:
                 e.activation(o, i, AF.Identity),
                 reads=[bq.name], writes=["lata"], note=f"latc{t}_{lc}")
        kpend = None
        for kc2 in range(2):
            bq = PS[4 + (kc2 % 2)]
            for lc in range(2):
                mm(kup[:, LAT * lc + 128 * kc2: LAT * lc + 128 * (kc2 + 1)],
                   lata[:, 512 * lc:512 * (lc + 1)], bq[:, :],
                   start=(lc == 0), stop=(lc == 1),
                   reads=["kup", "lata"], bank=bq, note=f"kp{t}_{kc2}{lc}")
            if kpend is not None:
                rope_qknorm_finish(*kpend)
            rope_qknorm_start(bq, cosg[:, 512 * t:512 * (t + 1)],
                              sing[:, 512 * t:512 * (t + 1)], ("cosg", "sing"),
                              f"k{t}_{kc2}", kc2 % 2)
            kpend = (kg[:, S * kc2 + 512 * t: S * kc2 + 512 * (t + 1)], "kg",
                     f"k{t}_{kc2}", kc2 % 2)
        rope_qknorm_finish(*kpend)
        for vc in range(2):
            bq = PS[5]
            for lc in range(2):
                mm(vup[:, LAT * lc + 128 * vc: LAT * lc + 128 * (vc + 1)],
                   lata[:, 512 * lc:512 * (lc + 1)], bq[:, :],
                   start=(lc == 0), stop=(lc == 1),
                   reads=["vup", "lata"], bank=bq, note=f"vp{t}_{vc}{lc}")
            s_op(lambda e, o=vtmp[:, 0:512], i=bq[:, :]: e.activation(o, i, AF.Identity),
                 reads=[bq.name], writes=["vtmp"], note=f"vc{t}_{vc}")
            for sb in range(4):
                em.op("tensor",
                      lambda e, o=PT[:, 128 * sb:128 * (sb + 1)],
                      i=vtmp[:, 128 * sb:128 * (sb + 1)], idn=idb[:]: e.transpose(o, i, idn),
                      reads=["vtmp", "idb"], writes=["PT"], note=f"vt{t}_{vc}{sb}")
                kc = 4 * t + sb
                for half in range(2):
                    kh = 2 * vc + half
                    v_op(lambda e, o=vg[:, 260 * kc + 65 * kh: 260 * kc + 65 * kh + 64],
                         i=PT[:, 128 * sb + 64 * half: 128 * sb + 64 * half + 64]:
                         e.tensor_copy(o, i),
                         reads=["PT"], writes=["vg"], note=f"ve{t}_{vc}{sb}{half}")

    if "y" in dbg or "t1" in dbg or "q" in dbg or "kg" in dbg or "vg" in dbg:
        if "t1" in dbg:
            dap = dbg_tensor("t1", [128, NCH * 512], BF16)
            em.dma("sync", dap, t1a[:], "dbg", reads=["t1a"], writes=[], note="dbg_t1")
        if "kg" in dbg:
            dap = dbg_tensor("kg", [128, 2 * S], BF16)
            em.dma("sync", dap, kg[:], "dbg", reads=["kg"], writes=[], note="dbg_kg")
        if "vg" in dbg:
            dap = dbg_tensor("vg", [128, NKC * 260], BF16)
            em.dma("sync", dap, vg[:], "dbg", reads=["vg"], writes=[], note="dbg_vg")

    em.barrier()
    pha.free()

    def attention(t, qt_b, kbuf, vbuf, kb_name, vb_name, local, tag,
                  ot_tile, tmpb, pbuf):
        if local:
            kc_lo = max(0, 4 * t - 2)
        else:
            kc_lo = 0
        kc_hi = 4 * t + 4
        npend = None

        def _norm_fin(p):
            sA, sB, nA, nB, j_ = p
            mm(onesall[64:65, 0:64], sA[64:65, 0:512], PS[6][0:64, :], True, True,
               reads=["onesall", nA], bank=PS[6],
               tile_position=(64, 0), note=f"nbA{tag}{t}_{j_}")
            mm(onesall[64:65, 0:64], sB[64:65, 0:512], PS[3][0:64, :], True, True,
               reads=["onesall", nB], bank=PS[3],
               tile_position=(64, 0), note=f"nbB{tag}{t}_{j_}")
            v_op(lambda e, o=ot_tile[0:64, 512 * j_:512 * (j_ + 1)], a=sA[0:64, 0:512],
                 b=PS[6][0:64, :]: e.tensor_mul(o, a, b),
                 reads=[nA, "P6"], writes=[ot_tile.name], note=f"onA{tag}{t}_{j_}")
            v_op(lambda e, o=tmpb[0:64, 512 * j_:512 * (j_ + 1)], a=sB[0:64, 0:512],
                 b=PS[3][0:64, :]: e.tensor_mul(o, a, b),
                 reads=[nB, "P3"], writes=[tmpb.name], note=f"onB{tag}{t}_{j_}")

        for j in range(8):
            kvA = 0 if j < 4 else 2
            kvB = 1 if j < 4 else 3
            pending = []
            def flush_pending():
                for (pb_, kv_, kc_, par_) in pending:
                    bo = PS[4 + par_]
                    mm(vbuf[:, 260 * kc_ + 65 * kv_: 260 * kc_ + 65 * (kv_ + 1)],
                       pb_[:, 0:512], bo[0:65, 0:512],
                       start=(kc_ == kc_lo), stop=(kc_ == kc_hi - 1),
                       reads=[vb_name, pb_.name], bank=bo,
                       note=f"o{tag}{t}_{j}{kc_}{par_}")
                pending.clear()
            for kc in range(kc_lo, kc_hi):
                jj = kc - 4 * t
                if local:
                    c0, c1 = max(0, 128 * jj), min(512, 128 * jj + 384)
                else:
                    c0, c1 = max(0, 128 * jj), 512
                masked = local or jj >= 0
                for parity, kv in ((0, kvA), (1, kvB)):
                    bs = PS[parity * 2 + ((kc - kc_lo) % 2)]
                    base = 64 * parity
                    if masked:
                        mi = maskoff + jj
                        mm(idb[:], maskt[:, 512 * mi:512 * (mi + 1)], bs[:, :],
                           True, False, reads=["idb", maskname], bank=bs,
                           note=f"mk{tag}{t}_{j}{kc}{parity}")
                    mm(kbuf[base:base + 64, S * (kv // 2) + 128 * kc: S * (kv // 2) + 128 * (kc + 1)],
                       qt_b[base:base + 64, 512 * j + c0: 512 * j + c1],
                       bs[:, c0:c1], not masked, True,
                       reads=[kb_name, qt_b.name], bank=bs,
                       tile_position=(base, 0), note=f"s{tag}{t}_{j}{kc}{parity}")
                    pb = pbuf[parity * 4 + ((kc - kc_lo) % 4)]
                    s_op(lambda e, o=pb[:, :], i=bs[:, :]:
                         e.activation(o, i, AF.Exp, scale=0.125),
                         reads=[bs.name], writes=[pb.name], note=f"e{tag}{t}_{j}{kc}{parity}")
                    pending.append((pb, kv, kc, parity))
                if len(pending) >= 8:
                    flush_pending()
            flush_pending()
            if npend is not None:
                _norm_fin(npend)
            sA = sqa[0] if j % 2 == 0 else pcav[0]
            sB = sqa[1] if j % 2 == 0 else psav[0]
            scA = pcav[0] if j % 2 == 0 else sqa[0]
            scB = psav[0] if j % 2 == 0 else sqa[1]
            nA = em.alias.get(sA.name, sA.name)
            nB = em.alias.get(sB.name, sB.name)
            s_op(lambda e, o=sA[0:65, 0:512], i=PS[4][0:65, :]:
                 e.activation(o, i, AF.Identity),
                 reads=["P4"], writes=[nA], note=f"ocA{tag}{t}_{j}")
            s_op(lambda e, o=sB[0:65, 0:512], i=PS[5][0:65, :]:
                 e.activation(o, i, AF.Identity),
                 reads=["P5"], writes=[nB], note=f"ocB{tag}{t}_{j}")
            recip_any(sA[64:65, 0:512], sA[64:65, 0:512], scA[64:65, 0:512],
                      [nA], nA, em.alias.get(scA.name, scA.name), f"A{tag}{t}{j}")
            recip_any(sB[64:65, 0:512], sB[64:65, 0:512], scB[64:65, 0:512],
                      [nB], nB, em.alias.get(scB.name, scB.name), f"B{tag}{t}{j}")
            npend = (sA, sB, nA, nB, j)
        _norm_fin(npend)
        em.dma("sync", ot_tile[64:128, :], tmpb[0:64, :], f"sh{tag}{t}",
               reads=[tmpb.name], writes=[ot_tile.name], note=f"shift{tag}{t}")

    phb = nc.auto_arena()
    wog = _sb(phb, "wog", [128, NCH * D], BF16)
    wql = _sb(phb, "wql", [128, NCH * D], BF16)
    wkl = _sb(phb, "wkl", [128, NCH * LAT], BF16)
    wvl = _sb(phb, "wvl", [128, NCH * LAT], BF16)
    cosl = _sb(phb, "cosl", [128, S], F32)
    sinl = _sb(phb, "sinl", [128, S], F32)
    rowsB = _sb(phb, "rowsB", [1, 2560], F32)
    rowsbBP = [_sb(phb, f"rowsbB{i}", [2, 1536], F32) for i in range(2)]
    rowsbB = rowsbBP[0]
    idbB = _sb(phb, "idbB", [128, 128], BF16)
    maskgB = _sb(phb, "maskgB", [128, 4 * 512], BF16)
    qtb = [_sb(phb, "qtb0", [128, NCH * 512], BF16)]
    ytb = [_sb(phb, "ytb0", [128, NCH * 512], F32)]
    otb = _sb(phb, "otb", [128, NCH * 512], BF16)
    tmpb = _sb(phb, "tmpb", [64, NCH * 512], BF16)
    pbufb = [_sb(phb, f"pb{i}", [128, 512], BF16) for i in range(8)]
    t2b = _sb(phb, "t2b", [128, NCH * 512], BF16)
    sqb = [_sb(phb, f"sqb{i}", [128, 512], F32) for i in range(2)]
    pcbv = [_sb(phb, f"pcb{i}", [128, 512], F32) for i in range(2)]
    psbv = [_sb(phb, f"psb{i}", [128, 512], F32) for i in range(2)]
    qsqbv = [_sb(phb, f"qsqb{i}", [128, 512], BF16) for i in range(2)]
    rbt2B = [_sb(phb, f"rbt2B{i}", [2, 1024], BF16) for i in range(2)]
    em.alias[rbt2B[0].name] = "rowsb2_0"
    em.alias[rbt2B[1].name] = "rowsb2_1"
    pcb, psb = pcbv[0], psbv[0]
    qlb = [_sb(phb, f"qlb{i}", [128, 512], BF16) for i in range(2)]
    phb.allocate()

    sqa = sqb
    pca, psa, qba = pcb, psb, qlb
    pcav, psav, qsqv = pcbv, psbv, qsqbv
    rbt2 = rbt2B
    rows, rowsb, idb = rowsB, rowsbB, idbB
    rowsbP = rowsbBP
    em.alias[rowsB.name] = "rows"
    em.alias[rowsbBP[0].name] = "rowsb0"
    em.alias[rowsbBP[1].name] = "rowsb1"
    em.alias[idbB.name] = "idb"
    for _i in range(2):
        em.alias[pcbv[_i].name] = "pca" + str(_i)
        em.alias[psbv[_i].name] = "psa" + str(_i)
    em.alias[maskgB.name] = "maskg"
    maskt, maskname, maskoff = maskgB, "maskg", 0

    wload(wog, wo_g_in, D, D, "wB", note="wog")
    wload(wql, wq_l_in, D, D, "wB", note="wql")
    wload(wkl, wk_l_in, D, LAT, "wB", note="wkl")
    wload(wvl, wv_l_in, D, LAT, "wB", note="wvl")
    ld(cosl, cosl_in[:], "c_ld", "cosl")
    ld(sinl, sinl_in[:], "c_ld", "sinl")
    ld(idbB, idb_in[:], "c_ld", "idbB")
    em.dma("sync", maskgB[:].rearrange("p (c n) -> p c n", c=4),
           maskg_in.rearrange("c p n -> p c n"), "s_maskgB",
           reads=[], writes=["maskg"], note="maskgB")

    def q_load(t, slot, src, sem):
        return em.dma("sync", qtb[slot][:].rearrange("p (c n) -> p c n", c=NCH),
                      src[t].rearrange("c p n -> p c n"), sem,
                      reads=[src.tensor.name], writes=[qtb[slot].name], note=f"qld{t}")

    def y_load(t, slot, src, sem):
        return em.dma("sync", ytb[slot][:].rearrange("p (c n) -> p c n", c=NCH),
                      src[t].rearrange("c p n -> p c n"), sem,
                      reads=[src.tensor.name], writes=[ytb[slot].name], note=f"yld{t}")

    for t in range(NT):
        q_load(t, 0, q_sp, "qldB0")
        y_load(t, 0, y_sp, "yldB0")
        qt = qtb[0]
        yt = ytb[0]
        attention(t, qt, kg, vg, "kg", "vg", False, "G", otb, tmpb, pbufb)
        for oc in range(NCH):
            bg = PS[0 + (oc % 2)]
            for c in range(NCH):
                mm(wog[:, D * c + 128 * oc: D * c + 128 * (oc + 1)],
                   otb[:, 512 * c:512 * (c + 1)], bg[:, :],
                   start=(c == 0), stop=(c == NCH - 1),
                   reads=["wog", "otb"], bank=bg, note=f"wog{t}_{oc}{c}")
            v_op(lambda e, o=yt[:, 512 * oc:512 * (oc + 1)], g=bg[:, :]:
                 e.tensor_add(o, o, g),
                 reads=[bg.name, yt.name], writes=[yt.name], note=f"y2_{t}{oc}")
        em.dma("gpsimd", y2_sp[t].rearrange("c p n -> p c n"),
               yt[:].rearrange("p (c n) -> p c n", c=NCH),
               f"y2sp{t}", reads=[yt.name], writes=["y2_sp"], note=f"y2sp{t}")
        sum_over_chunks(yt, PS[6], True, f"sy2{t}")
        R = rms_row(PS[6], EPS_RMS, 0, f"rms2_{t}")
        bcast(R, PS[2], ["rows"], f"bR2{t}")
        for c in range(NCH):
            v_op(lambda e, o=t2b[:, 512 * c:512 * (c + 1)], y=yt[:, 512 * c:512 * (c + 1)],
                 w=n2w[:, c:c + 1], r=PS[2][:, :]:
                 e.scalar_tensor_tensor(o, y, w, r, AL.mult, AL.mult),
                 reads=[yt.name, "n2w", "P2"], writes=["t2b"], note=f"t2_{t}{c}")
        pend = None
        for oc in range(NCH):
            bq = PS[4 + (oc % 2)]
            for c in range(NCH):
                mm(wql[:, D * c + 128 * oc: D * c + 128 * (oc + 1)],
                   t2b[:, 512 * c:512 * (c + 1)], bq[:, :],
                   start=(c == 0), stop=(c == NCH - 1),
                   reads=["wql", "t2b"], bank=bq, note=f"qlp{t}_{oc}{c}")
            def _flush_ql(p):
                qb_, tg_, par_, oc_ = p
                rope_qknorm_finish(qb_[:, 0:512], qb_.name, tg_, par_)
                em.dma("gpsimd", ql_sp[t, oc_], qb_[:, 0:512], f"qlsp{t}",
                       reads=[qb_.name], writes=["ql_sp"], note=f"qlsp{t}_{oc_}")
            if pend is not None:
                _flush_ql(pend)
            rope_qknorm_start(bq, cosl[:, 512 * t:512 * (t + 1)],
                              sinl[:, 512 * t:512 * (t + 1)], ("cosl", "sinl"),
                              f"ql{t}_{oc}", oc % 2)
            pend = (qlb[oc % 2], f"ql{t}_{oc}", oc % 2, oc)
        _flush_ql(pend)
        klpend = None
        for kc2 in range(2):
            bq = PS[4 + (kc2 % 2)]
            for c in range(NCH):
                mm(wkl[:, LAT * c + 128 * kc2: LAT * c + 128 * (kc2 + 1)],
                   t2b[:, 512 * c:512 * (c + 1)], bq[:, :],
                   start=(c == 0), stop=(c == NCH - 1),
                   reads=["wkl", "t2b"], bank=bq, note=f"klp{t}_{kc2}{c}")
            if klpend is not None:
                rope_qknorm_finish(*klpend)
            rope_qknorm_start(bq, cosl[:, 512 * t:512 * (t + 1)],
                              sinl[:, 512 * t:512 * (t + 1)], ("cosl", "sinl"),
                              f"kl{t}_{kc2}", kc2 % 2)
            klpend = (kl[:, S * kc2 + 512 * t: S * kc2 + 512 * (t + 1)], "kl",
                      f"kl{t}_{kc2}", kc2 % 2)
        rope_qknorm_finish(*klpend)
        for vc in range(2):
            bq = PS[5]
            for c in range(NCH):
                mm(wvl[:, LAT * c + 128 * vc: LAT * c + 128 * (vc + 1)],
                   t2b[:, 512 * c:512 * (c + 1)], bq[:, :],
                   start=(c == 0), stop=(c == NCH - 1),
                   reads=["wvl", "t2b"], bank=bq, note=f"vlp{t}_{vc}{c}")
            s_op(lambda e, o=sqb[1][:, 0:512], i=bq[:, :]: e.activation(o, i, AF.Identity),
                 reads=[bq.name], writes=[sqb[1].name], note=f"vlc{t}_{vc}")
            v_op(lambda e, o=pbufb[3][:, 0:512], i=sqb[1][:, 0:512]: e.tensor_copy(o, i),
                 reads=[sqb[1].name], writes=[pbufb[3].name], note=f"vlcb{t}_{vc}")
            for sb in range(4):
                em.op("tensor",
                      lambda e, o=PT[:, 128 * sb:128 * (sb + 1)],
                      i=pbufb[3][:, 128 * sb:128 * (sb + 1)], idn=idb[:]: e.transpose(o, i, idn),
                      reads=[pbufb[3].name, "idb"], writes=["PT"], note=f"vlt{t}_{vc}{sb}")
                kc = 4 * t + sb
                for half in range(2):
                    kh = 2 * vc + half
                    v_op(lambda e, o=vl[:, 260 * kc + 65 * kh: 260 * kc + 65 * kh + 64],
                         i=PT[:, 128 * sb + 64 * half: 128 * sb + 64 * half + 64]:
                         e.tensor_copy(o, i),
                         reads=["PT"], writes=["vl"], note=f"vle{t}_{vc}{sb}{half}")

    em.barrier()
    phb.free()

    phc = nc.auto_arena()
    wol = _sb(phc, "wol", [128, NCH * D], BF16)
    rowsC = _sb(phc, "rowsC", [1, 2560], F32)
    rowsbCP = [_sb(phc, f"rowsbC{i}", [2, 64], F32) for i in range(2)]
    rowsbC = rowsbCP[0]
    idfC = _sb(phc, "idfC", [128, 128], F32)
    idbC = _sb(phc, "idbC", [128, 128], BF16)
    masklC = _sb(phc, "masklC", [128, 6 * 512], BF16)
    c_all = _sb(phc, "c_all", [128, (S // 128) * NE], F32)
    qtc = [_sb(phc, f"qtc{i}", [128, NCH * 512], BF16) for i in range(2)]
    xtc = [_sb(phc, f"xtc{i}", [128, NCH * 512], F32) for i in range(2)]
    ytc = [_sb(phc, f"ytc{i}", [128, NCH * 512], F32) for i in range(2)]
    otc = _sb(phc, "otc", [128, NCH * 512], BF16)
    tmpc = _sb(phc, "tmpc", [64, NCH * 512], BF16)
    pbufc = [_sb(phc, f"pc{i}", [128, 512], BF16) for i in range(8)]
    ztc = _sb(phc, "ztc", [128, NCH * 512], BF16)
    sqc = [_sb(phc, f"sqc{i}", [128, 512], F32) for i in range(2)]
    y1c = _sb(phc, "y1c", [128, 512], F32)
    pcc = _sb(phc, "pcc", [128, 512], F32)
    pscc = _sb(phc, "pscc", [128, 512], F32)
    lgc = _sb(phc, "lgc", [128, 16 * 4], F32)
    phc.allocate()

    sqa = sqc
    rows, rowsb, idf = rowsC, rowsbC, idfC
    rowsbP = rowsbCP
    pca, psa = pcc, pscc
    pcav = [pcc, pcc]
    psav = [pscc, pscc]
    em.alias[pcc.name] = "pca0"
    em.alias[pscc.name] = "psa0"
    em.alias[rowsC.name] = "rows"
    em.alias[rowsbCP[0].name] = "rowsb0"
    em.alias[rowsbCP[1].name] = "rowsb1"
    em.alias[idfC.name] = "idf"
    em.alias[masklC.name] = "maskl"
    em.alias[idbC.name] = "idb"
    idb = idbC
    maskt, maskname, maskoff = masklC, "maskl", 2

    wload(wol, wo_l_in, D, D, "wC", note="wol")
    ld(idfC, idf_in[:], "c_ld", "idfC")
    ld(idbC, idb_in[:], "c_ld", "idbC")
    em.dma("sync", masklC[:].rearrange("p (c n) -> p c n", c=6),
           maskl_in.rearrange("c p n -> p c n"), "s_masklC",
           reads=[], writes=["maskl"], note="masklC")

    def qc_load(t, slot):
        return em.dma("sync", qtc[slot][:].rearrange("p (c n) -> p c n", c=NCH),
                      ql_sp[t].rearrange("c p n -> p c n"), f"qldC{slot}",
                      reads=["ql_sp"], writes=[qtc[slot].name], note=f"qcl{t}")

    def xc_load(t, slot):
        return em.dma("sync", xtc[slot][:].rearrange("p (c n) -> p c n", c=NCH),
                      x_in.rearrange("(c p) n -> p c n", p=128)[:, :, 512 * t:512 * t + 512],
                      f"xldC{slot}", reads=[], writes=[xtc[slot].name], note=f"xcl{t}")

    def yc_load(t, slot):
        return em.dma("sync", ytc[slot][:].rearrange("p (c n) -> p c n", c=NCH),
                      y2_sp[t].rearrange("c p n -> p c n"), f"yldC{slot}",
                      reads=["y2_sp"], writes=[ytc[slot].name], note=f"ycl{t}")

    qc_load(0, 0)
    xc_load(0, 0)
    yc_load(0, 0)

    for t in range(NT):
        if t + 1 < NT:
            qc_load(t + 1, (t + 1) % 2)
            xc_load(t + 1, (t + 1) % 2)
            yc_load(t + 1, (t + 1) % 2)
        qt = qtc[t % 2]
        xt = xtc[t % 2]
        yt = ytc[t % 2]
        attention(t, qt, kl, vl, "kl", "vl", True, "L", otc, tmpc, pbufc)
        for c in range(NCH):
            v_op(lambda e, o=xt[:, 512 * c:512 * (c + 1)], y=yt[:, 512 * c:512 * (c + 1)]:
                 e.tensor_add(o, o, y),
                 reads=[yt.name, xt.name], writes=[xt.name], note=f"xy{t}{c}")
        for oc in range(NCH):
            bg = PS[0 + (oc % 2)]
            for c in range(NCH):
                mm(wol[:, D * c + 128 * oc: D * c + 128 * (oc + 1)],
                   otc[:, 512 * c:512 * (c + 1)], bg[:, :],
                   start=(c == 0), stop=(c == NCH - 1),
                   reads=["wol", "otc"], bank=bg, note=f"wol{t}_{oc}{c}")
            v_op(lambda e, o=xt[:, 512 * oc:512 * (oc + 1)], g=bg[:, :]:
                 e.tensor_add(o, o, g),
                 reads=[bg.name, xt.name], writes=[xt.name], note=f"xn_{t}{oc}")
        em.dma("gpsimd", xn_sp[t].rearrange("c p n -> p c n"),
               xt[:].rearrange("p (c n) -> p c n", c=NCH),
               f"xnsp{t}", reads=[xt.name], writes=["xn_sp"], note=f"xnsp{t}")
        sum_over_chunks(xt, PS[6], False, f"zx{t}")
        ln_mu(PS[6], 0, f"ln2_{t}")
        sum_over_chunks(xt, PS[6], True, f"zxx{t}")
        A, B = ln_rows(PS[6], EPS_LN, 0, f"ln2_{t}")
        bcast(A, PS[2], ["rows"], f"zA{t}")
        bcast(B, PS[3], ["rows"], f"zB{t}")
        for c in range(NCH):
            v_op(lambda e, o=y1c[:, 0:512], x=xt[:, 512 * c:512 * (c + 1)], a=PS[2][:, :]:
                 e.tensor_mul(o, x, a),
                 reads=[xt.name, "P2"], writes=["y1c"], note=f"z1_{t}{c}")
            v_op(lambda e, o=ztc[:, 512 * c:512 * (c + 1)], a=y1c[:, 0:512], b=PS[3][:, :]:
                 e.tensor_add(o, a, b),
                 reads=["y1c", "P3"], writes=["ztc"], note=f"z2_{t}{c}")
        em.dma("gpsimd", z_sp[t].rearrange("c p n -> p c n"),
               ztc[:].rearrange("p (c n) -> p c n", c=NCH),
               f"zsp{t}", reads=["ztc"], writes=["z_sp"], note=f"zsp{t}")
        for tc in range(4):
            bq = PS[4]
            for c in range(NCH):
                mm(ztc[:, 512 * c + 128 * tc: 512 * c + 128 * (tc + 1)],
                   rwb[:, NE * c: NE * (c + 1)], bq[0:128, 0:NE],
                   start=(c == 0), stop=False,
                   reads=["ztc", "rwb"], bank=bq, note=f"rt{t}{tc}{c}")
            mm(ones128[:], rbb[:], bq[0:128, 0:NE], start=False, stop=True,
               reads=["ones128", "rbb"], bank=bq, note=f"rtb{t}{tc}")
            lg = lgc[:, 16 * tc: 16 * tc + 16]
            v_op(lambda e, o=lg, i=bq[0:128, 0:NE]: e.tensor_copy(o, i),
                 reads=[bq.name], writes=["lgc"], note=f"lgc{t}{tc}")
            mx8 = sqc[0][:, 0:8]
            v_op(lambda e, o=mx8, i=lg: e.max(o, i),
                 reads=["lgc"], writes=[sqc[0].name], note=f"mx{t}{tc}")
            v_op(lambda e, o=sqc[0][:, 4:8]: e.memset(o, -2e30),
                 reads=[], writes=[sqc[0].name], note=f"mz{t}{tc}")
            wk = sqc[0][:, 8:24]
            v_op(lambda e, o=wk, m=sqc[0][:, 0:8], i=lg: e.match_replace(o, m, i, -1e30),
                 reads=[sqc[0].name, "lgc"], writes=[sqc[0].name], note=f"mr{t}{tc}")
            sel = sqc[0][:, 24:40]
            v_op(lambda e, o=sel, a=wk, b=lg: e.tensor_tensor(o, a, b, AL.not_equal),
                 reads=[sqc[0].name, "lgc"], writes=[sqc[0].name], note=f"sel{t}{tc}")
            nmx = sqc[0][:, 40:41]
            v_op(lambda e, o=nmx, i=sqc[0][:, 0:1]: e.tensor_scalar(o, i, -1.0, None, AL.mult),
                 reads=[sqc[0].name], writes=[sqc[0].name], note=f"nmx{t}{tc}")
            eg = sqc[0][:, 48:64]
            s_op(lambda e, o=eg, i=lg, b=nmx: e.activation(o, i, AF.Exp, bias=b),
                 reads=["lgc", sqc[0].name], writes=[sqc[0].name], note=f"eg{t}{tc}")
            es_ = sqc[0][:, 64:80]
            v_op(lambda e, o=es_, a=eg, b=sel: e.tensor_mul(o, a, b),
                 reads=[sqc[0].name], writes=[sqc[0].name], note=f"es{t}{tc}")
            ssum = sqc[0][:, 80:81]
            v_op(lambda e, o=ssum, i=es_: e.reduce_sum(o, i, axis=AX.X),
                 reads=[sqc[0].name], writes=[sqc[0].name], note=f"ss{t}{tc}")
            rs = sqc[0][:, 81:82]
            v_op(lambda e, o=rs, i=ssum: e.reciprocal(o, i),
                 reads=[sqc[0].name], writes=[sqc[0].name], note=f"rs{t}{tc}")
            gtc = 4 * t + tc
            v_op(lambda e, o=c_all[:, NE * gtc: NE * (gtc + 1)], a=es_, b=rs:
                 e.tensor_scalar(o, a, b, None, AL.mult),
                 reads=[sqc[0].name], writes=["c_all"], note=f"c{t}{tc}")
            em.op("tensor",
                  lambda e, o=PS[5][0:16, 0:128], i=c_all[:, NE * gtc: NE * (gtc + 1)],
                  idn=idf[:]: e.transpose(o, i, idn),
                  reads=["c_all", "idf"], writes=["P5"], note=f"ct{t}{tc}")
            v_op(lambda e, o=cfm[0:16, 128 * gtc:128 * (gtc + 1)], i=PS[5][0:16, 0:128]:
                 e.tensor_copy(o, i),
                 reads=["P5"], writes=["cfm"], note=f"ce{t}{tc}")

    em.barrier()
    phc.free()

    phf = nc.auto_arena()
    wbuf = [_sb(phf, f"wf{i}", [128, NCH * FF], BF16) for i in range(2)]
    NIC = FF // 128
    NTH = HTOK // 512
    hbuf = _sb(phf, "hbuf", [128, NIC * NTH * 512], BF16)
    zbuf = [_sb(phf, f"zf{i}", [128, NCH * 512], BF16) for i in range(NTH)]
    accf = _sb(phf, "accf", [128, NCH * HTOK], F32)
    cbc = [_sb(phf, f"cbc{i}", [128, 512], F32) for i in range(2)]
    tmpf = [_sb(phf, f"tmpf{i}", [128, 512], F32) for i in range(2)]
    crow2 = _sb(phf, "crow2", [1, 1024], F32)
    phf.allocate()

    def w1_load(e, slot):
        return wload(wbuf[slot], w1_in[e], D, FF, f"wf{slot}", note=f"w1_{e}")

    def w2_load(e, slot):
        return wload(wbuf[slot], w2_in[e], FF, D, f"wf{slot}", note=f"w2_{e}")

    def z_load(gt, slot):
        return em.dma("sync", zbuf[slot][:].rearrange("p (c n) -> p c n", c=NCH),
                      z_sp[gt].rearrange("c p n -> p c n"), f"zld{slot}",
                      reads=["z_sp"], writes=[zbuf[slot].name], note=f"zl{gt}")

    def acc_load(hf):
        t0 = hf * NTH
        for tt in range(NTH):
            em.dma("sync",
                   accf[:, :].rearrange("p (c h n) -> p c h n", c=NCH, h=NTH)[:, :, tt, :],
                   xn_sp[t0 + tt].rearrange("c p n -> p c n"),
                   "accld", reads=["xn_sp"], writes=["accf"], note=f"accl{hf}{tt}")

    for hf in range(NHALF):
        acc_load(hf)
        w1_load(0, 0)
        w2_load(0, 1)
        for e in range(NE):
            em.dma("sync", crow2[0:1, 0:HTOK],
                   cfm[e:e + 1, hf * HTOK:(hf + 1) * HTOK], f"crw{e % 2}",
                   reads=["cfm"], writes=["crow2"], note=f"crw{hf}{e}")
            for tt in range(NTH):
                gt = hf * NTH + tt
                if e == 0:
                    z_load(gt, tt)
                zb = zbuf[tt]
                for icc in range(NIC):
                    bh = PS[icc % 4]
                    for c in range(NCH):
                        mm(wbuf[0][:, FF * c + 128 * icc: FF * c + 128 * (icc + 1)],
                           zb[:, 512 * c:512 * (c + 1)], bh[:, :],
                           start=(c == 0), stop=(c == NCH - 1),
                           reads=[wbuf[0].name, zb.name], bank=bh,
                           note=f"m1_{hf}{e}{tt}{icc}{c}")
                    hcol = (icc * NTH + tt) * 512
                    s_op(lambda e_, o=hbuf[:, hcol:hcol + 512], i=bh[:, :],
                         b=b1b[:, NIC * e + icc: NIC * e + icc + 1]:
                         e_.activation(o, i, AF.Gelu, bias=b),
                         reads=[bh.name, "b1b"], writes=["hbuf"], note=f"g{hf}{e}{tt}{icc}")
            if e + 1 < NE:
                w1_load(e + 1, 0)
            for tt in range(NTH):
                gt512 = 512 * tt
                mm(ones128[:], crow2[0:1, gt512:gt512 + 512], PS[6][:, :],
                   True, True, reads=["ones128", "crow2"], bank=PS[6],
                   note=f"cb{hf}{e}{tt}")
                s_op(lambda e_, o=cbc[tt % 2][:, :], i=PS[6][:, :]:
                     e_.activation(o, i, AF.Identity),
                     reads=["P6"], writes=[cbc[tt % 2].name], note=f"cbe{hf}{e}{tt}")
                for fc in range(NCH):
                    bo = [PS[4], PS[5], PS[0], PS[1]][fc % 4]
                    for icc in range(NIC):
                        hcol = (icc * NTH + tt) * 512
                        mm(wbuf[1][:, D * icc + 128 * fc: D * icc + 128 * (fc + 1)],
                           hbuf[:, hcol:hcol + 512], bo[:, :],
                           start=(icc == 0), stop=(icc == NIC - 1),
                           reads=[wbuf[1].name, "hbuf"], bank=bo,
                           note=f"m2_{hf}{e}{tt}{fc}{icc}")
                    tf = tmpf[fc % 2]
                    v_op(lambda e_, o=tf[:, :], a=bo[:, :], b=cbc[tt % 2][:, :]:
                         e_.tensor_mul(o, a, b),
                         reads=[bo.name, cbc[tt % 2].name], writes=[tf.name],
                         note=f"fm{hf}{e}{tt}{fc}")
                    acc_sl = accf[:, HTOK * fc + 512 * tt: HTOK * fc + 512 * (tt + 1)]
                    v_op(lambda e_, o=acc_sl, a=tf[:, :]: e_.tensor_add(o, o, a),
                         reads=[tf.name, "accf"], writes=["accf"],
                         note=f"fa{hf}{e}{tt}{fc}")
            if e + 1 < NE:
                w2_load(e + 1, 1)
        for tt in range(NTH):
            gt512 = hf * HTOK + tt * 512
            for fc in range(NCH):
                bo = [PS[4], PS[5], PS[0], PS[1]][fc % 4]
                mm(b2b[:, 128 * fc:128 * (fc + 1)], cfm[0:16, gt512:gt512 + 512],
                   bo[:, :], True, True, reads=["b2b", "cfm"], bank=bo,
                   note=f"bc{hf}{tt}{fc}")
                acc_sl = accf[:, HTOK * fc + 512 * tt: HTOK * fc + 512 * (tt + 1)]
                v_op(lambda e_, o=acc_sl, a=bo[:, :]: e_.tensor_add(o, o, a),
                     reads=[bo.name, "accf"], writes=["accf"], note=f"ba{hf}{tt}{fc}")
        em.dma("gpsimd",
               out_t.rearrange("(c p) n -> p c n", p=128)[:, :, hf * HTOK:(hf + 1) * HTOK],
               accf[:].rearrange("p (c n) -> p c n", c=NCH),
               "outst", reads=["accf"], writes=["out"], note=f"out{hf}")
        if hf + 1 < NHALF:
            em.barrier()

    fin = em.op("sync", lambda e: e.nop(), reads=["out"], writes=[], note="fin")

    sem_names = sorted(em.sems.keys())
    with contextlib.ExitStack() as ses:
        for sn in sem_names:
            em.sem_handles[sn] = ses.enter_context(nc.semaphore(sn))
        with nc.Block() as block:
            em.replay(block)

    es.close()
    nc._em = em
    return nc, dbg_outs



def _prep_shared(inputs, S):
    pq = _perm_q()
    pk = _perm_k()
    po = _perm_o()
    f32 = np.float32
    d = {}
    d["g_wq_p"] = np.ascontiguousarray(inputs["L0_attn_g_wq"][:, pq], dtype=f32)
    d["g_kv_down"] = np.ascontiguousarray(inputs["L0_attn_g_kv_down"], dtype=f32)
    d["g_k_up_p"] = np.ascontiguousarray(inputs["L0_attn_g_k_up"][:, pk], dtype=f32)
    d["g_v_up"] = np.ascontiguousarray(inputs["L0_attn_g_v_up"], dtype=f32)
    d["g_wo_p"] = np.ascontiguousarray(inputs["L0_attn_g_wo"][po, :], dtype=f32)
    d["l_wq_p"] = np.ascontiguousarray(inputs["L0_attn_l_wq"][:, pq], dtype=f32)
    d["l_wk_p"] = np.ascontiguousarray(inputs["L0_attn_l_wk"][:, pk], dtype=f32)
    d["l_wv"] = np.ascontiguousarray(inputs["L0_attn_l_wv"], dtype=f32)
    d["l_wo_p"] = np.ascontiguousarray(inputs["L0_attn_l_wo"][po, :], dtype=f32)
    d["n1w"] = np.ascontiguousarray(
        np.asarray(inputs["L0_attn_norm1_w"], f32).reshape(NCH, 128).T)
    d["n2w"] = np.ascontiguousarray(
        np.asarray(inputs["L0_attn_norm2_w"], f32).reshape(NCH, 128).T)
    d["router_w"] = np.ascontiguousarray(inputs["L0_ffn_router_w"], dtype=f32)
    d["router_b"] = np.ascontiguousarray(
        np.asarray(inputs["L0_ffn_router_b"], f32).reshape(1, NE))
    d["w1"] = np.ascontiguousarray(inputs["L0_ffn_w1"], dtype=f32)
    d["b1r"] = np.ascontiguousarray(
        np.asarray(inputs["L0_ffn_b1"], f32).reshape(NE, FF // 128, 128)
        .transpose(2, 0, 1).reshape(128, NE * (FF // 128)))
    d["w2"] = np.ascontiguousarray(inputs["L0_ffn_w2"], dtype=f32)
    d["b2_all"] = np.ascontiguousarray(inputs["L0_ffn_b2"], dtype=f32)
    cg, sg = _rope_tables(S, BASE_G)
    cl, sl = _rope_tables(S, BASE_L)
    sign = np.where((np.arange(128) % 64) < 32, np.float32(-1.0), np.float32(1.0))
    d["cos_g"], d["sin_g"] = cg[:, :S], np.ascontiguousarray(sg[:, :S] * sign[:, None])
    d["cos_l"], d["sin_l"] = cl[:, :S], np.ascontiguousarray(sl[:, :S] * sign[:, None])
    psw = np.zeros((128, 128), np.float32)
    for r in range(128):
        b = (r // 64) * 64
        psw[r, b + (r - b + 32) % 64] = 1.0
    d["p_swap"] = psw
    tl, tu = _tri_masks()
    d["tri_l"], d["tri_u"] = tl, tu
    p_ = np.arange(128)[:, None]
    qf_ = np.arange(512)[None, :]
    mg = np.zeros((4, 128, 512), np.float32)
    for jj in range(4):
        mg[jj] = np.where(qf_ >= p_ + 128 * jj, 0.0, -1e30)
    mloc = np.zeros((6, 128, 512), np.float32)
    for ji, jj in enumerate(range(-2, 4)):
        dqk = qf_ - p_ - 128 * jj
        mloc[ji] = np.where((dqk >= 0) & (dqk < WIN), 0.0, -1e30)
    d["mask_g"] = mg.astype(ml_dtypes.bfloat16)
    d["mask_l"] = mloc.astype(ml_dtypes.bfloat16)
    d["ident_f32"] = np.eye(128, dtype=f32)
    d["ident_bf16"] = np.eye(128, dtype=ml_dtypes.bfloat16)
    d["ones_1x128"] = np.ones((1, 128), f32)
    d["ones_c128"] = np.ones((128, 1), f32)
    d["ones_all"] = np.ones((128, 128), f32)
    cp = np.zeros((128, 8), f32)
    cp[:, 0] = EPS_LN
    cp[:, 1] = EPS_RMS
    cp[:, 2] = 1.0 / D
    cp[:, 3] = 1.0 / HD
    cp[:, 4] = 0.125
    cp[:, 5] = -1.0
    cp[:, 6] = -0.5
    d["cpool"] = cp
    d["ones_1x64"] = np.ones((1, 64), f32)
    blk2 = np.zeros((128, 2), f32)
    blk2[0:64, 0] = 1.0
    blk2[64:128, 1] = 1.0
    d["blk2"] = blk2
    d["blk2b"] = blk2.astype(ml_dtypes.bfloat16)
    ind2 = np.zeros((2, 128), f32)
    ind2[0, 0:64] = 1.0
    ind2[1, 64:128] = 1.0
    d["ind2"] = ind2
    d["ind2b"] = ind2.astype(ml_dtypes.bfloat16)
    return d


def kernel(**inputs):
    hidden = np.asarray(inputs["L0_attn_hidden"], np.float32)
    B, S, _ = hidden.shape
    shared = _prep_shared(inputs, S)
    nc, _ = build_program(S=S)
    in_maps = []
    for b in range(B):
        m = dict(shared)
        m["x_fm"] = np.ascontiguousarray(hidden[b].T)
        in_maps.append(m)
    res = run_bass_kernel_spmd(nc, in_maps, core_ids=list(range(B)))
    out = np.stack([np.asarray(r["out"], np.float32).T for r in res.results])
    return np.ascontiguousarray(out)


# version 29
# speedup vs baseline: 1.1173x; 1.1173x over previous
import contextlib
import math
import numpy as np
import ml_dtypes

import concourse.bass as bass
import concourse.dve_ops as dve_ops
import concourse.mybir as mybir
from concourse.bass_utils import run_bass_kernel_spmd

F32 = mybir.dt.float32
BF16 = mybir.dt.bfloat16
AF = mybir.ActivationFunctionType
AL = mybir.AluOpType
AX = mybir.AxisListType

D = 1024
H = 16
KVH = 4
HD = 64
LAT = 256
WIN = 256
NE = 16
TOPK = 4
FF = 2048
NCH = D // 128
BASE_G, BASE_L = 1.0e6, 1.0e4
EPS_LN = 1e-5
EPS_RMS = 1e-6

SLOT_A = [0, 1, 2, 3, 8, 9, 10, 11]
SLOT_B = [4, 5, 6, 7, 12, 13, 14, 15]
EO = np.concatenate([np.arange(0, HD, 2), np.arange(1, HD, 2)])


def _perm_q():
    cols = []
    for j in range(8):
        cols.append(64 * SLOT_A[j] + EO)
        cols.append(64 * SLOT_B[j] + EO)
    return np.concatenate(cols)


def _perm_k():
    return np.concatenate([64 * kh + EO for kh in range(KVH)])


def _perm_o():
    rows = []
    for j in range(8):
        rows.append(64 * SLOT_A[j] + np.arange(64))
        rows.append(64 * SLOT_B[j] + np.arange(64))
    return np.concatenate(rows)


_INVF_BITS = {
    1.0e6: [0x3f800000, 0x3f263de0, 0x3ed7e89b, 0x3e8c3504, 0x3e361887,
            0x3dec7fd6, 0x3d99940d, 0x3d47763f, 0x3d0186e3, 0x3ca8398b,
            0x3c5a7bf2, 0x3c0de12d, 0x3bb8449c, 0x3b6f520e, 0x3b1b690d,
            0x3ac9d75c, 0x3a83126f, 0x3a2a3b44, 0x39dd1725, 0x398f9272,
            0x393a7753, 0x38f22ce2, 0x389d43a4, 0x384c3fbe, 0x3804a2b2,
            0x37ac431d, 0x375fba4f, 0x371148e3, 0x36bcb0c1, 0x36751070,
            0x361f23e4, 0x35ceaf79],
    1.0e4: [0x3f800000, 0x3f3ff911, 0x3f0ff59a, 0x3ed7e89b, 0x3ea1e89b,
            0x3e72d423, 0x3e361887, 0x3e088d77, 0x3dcccccd, 0x3d99940d,
            0x3d6655c2, 0x3d2cba15, 0x3d0186e3, 0x3cc2434f, 0x3c91ad39,
            0x3c5a7bf2, 0x3c23d70a, 0x3bf5b9b0, 0x3bb8449c, 0x3b8a2e77,
            0x3b4f3e38, 0x3b1b690d, 0x3ae91528, 0x3aaec98e, 0x3a83126f,
            0x3a44948c, 0x3a136a16, 0x39dd1725, 0x39a5cb60, 0x3978a815,
            0x393a7753, 0x390bd472],
}


def _rope_tables(S, base):
    invf = np.array(_INVF_BITS[base], dtype=np.uint32).view(np.float32)
    t = np.arange(S, dtype=np.float32)
    ang = (t[None, :] * invf[:, None]).astype(np.float32)
    cos = np.cos(ang).astype(np.float32)
    sin = np.sin(ang).astype(np.float32)
    cos128 = np.tile(cos, (4, 1))
    sin128 = np.tile(sin, (4, 1))
    return np.ascontiguousarray(cos128), np.ascontiguousarray(sin128)


def _tri_masks():
    p = np.arange(128)[:, None]
    q = np.arange(128)[None, :]
    tri_l = np.where(p > q, np.float32(-1e30), np.float32(0.0)).astype(np.float32)
    tri_u = np.where(p <= q, np.float32(-1e30), np.float32(0.0)).astype(np.float32)
    return tri_l, tri_u



class _Step:
    __slots__ = ("fn", "waits", "incs", "drain", "seq", "note", "_dma_val", "is_dma", "pos")

    def __init__(self, fn, seq, note):
        self.fn = fn
        self.waits = []
        self.incs = []
        self.drain = False
        self.seq = seq
        self.note = note
        self.is_dma = False
        self.pos = -1


class EM:
    ENGINES = ("tensor", "vector", "scalar", "gpsimd", "sync")

    def __init__(self, nc):
        self.nc = nc
        self.steps = {e: [] for e in self.ENGINES}
        self.seq = 0
        self.sems = {}
        self.sem_handles = {}
        for e in self.ENGINES:
            self.sems["tk_" + e] = 0
        self.incs = {e: [] for e in self.ENGINES}
        self.last_step = {e: None for e in self.ENGINES}
        self.last_drain = {e: -1 for e in self.ENGINES}
        self.waited = {e: {} for e in self.ENGINES}
        self.last_w = {}
        self.readers = {}
        self.alias = {}

    def new_dma_sem(self, name):
        if name not in self.sems:
            self.sems[name] = 0

    def _engine_token(self, eng, step):
        for (sq, val, st) in self.incs[eng]:
            if sq >= step.seq:
                assert st.pos >= step.pos, (
                    f"token attach bug: need pos {step.pos} ({step.note}) got "
                    f"pos {st.pos} ({st.note}) sq={sq} vs {step.seq}")
                return ("tk_" + eng, val)
        last = self.last_step[eng]
        assert last is not None
        if last.is_dma or last.fn is None or getattr(last, "note", "") in ("barrier", "cpw", "tok_drain"):
            if eng == "sync":
                last = _Step(lambda e: e.nop(), self.seq, "tok_drain")
            else:
                last = _Step(lambda e: e.drain(), self.seq, "tok_drain")
            self.seq += 1
            last.pos = len(self.steps[eng])
            self.steps[eng].append(last)
            self.last_step[eng] = last
            self.last_drain[eng] = last.seq
        self.sems["tk_" + eng] += 1
        val = self.sems["tk_" + eng]
        last.incs.append(("tk_" + eng, 1))
        self.incs[eng].append((last.seq, val, last))
        return ("tk_" + eng, val)

    def _token_of(self, prod):
        kind, step, eng = prod
        if kind == "eng":
            return self._engine_token(eng, step)
        else:
            return (kind, step._dma_val)

    def op(self, eng, fn, reads=(), writes=(), note="", extra_waits=()):
        step = _Step(fn, self.seq, note)
        self.seq += 1
        waits = list(extra_waits)
        no_drain = (eng == "tensor")
        rd = [self.alias.get(r if isinstance(r, str) else r.name,
                             r if isinstance(r, str) else r.name) for r in reads]
        wr = [self.alias.get(w if isinstance(w, str) else w.name,
                             w if isinstance(w, str) else w.name) for w in writes]
        for r in rd:
            p = self.last_w.get(r)
            if p is None:
                continue
            if p[0] == "eng" and p[2] == eng:
                if p[1].seq >= self.last_drain[eng] and not no_drain:
                    step.drain = True
            else:
                waits.append(self._token_of(p))
        for w in wr:
            p = self.last_w.get(w)
            if p is not None:
                if p[0] == "eng" and p[2] == eng:
                    if p[1].seq >= self.last_drain[eng] and not no_drain:
                        step.drain = True
                else:
                    waits.append(self._token_of(p))
            for rp in self.readers.get(w, ()):
                if rp[0] == "eng" and rp[2] == eng:
                    if rp[1].seq >= self.last_drain[eng] and not no_drain:
                        step.drain = True
                    continue
                waits.append(self._token_of(rp))
        wt = self.waited[eng]
        merged = {}
        for (s, v) in waits:
            if v > wt.get(s, 0) and v > merged.get(s, 0):
                merged[s] = v
        for s, v in merged.items():
            step.waits.append((s, v))
            wt[s] = v
        if step.drain:
            self.last_drain[eng] = step.seq
        step.pos = len(self.steps[eng])
        self.steps[eng].append(step)
        self.last_step[eng] = step
        me = ("eng", step, eng)
        for w in wr:
            self.last_w[w] = me
            self.readers[w] = []
        for r in rd:
            self.readers.setdefault(r, []).append(me)
        return step

    def dma(self, eng, out_ap, in_ap, sem, reads=(), writes=(), note=""):
        self.new_dma_sem(sem)
        step = self.op(eng, None, reads=reads, writes=writes, note=note)
        step.fn = lambda e, o=out_ap, i=in_ap: e.dma_start(out=o, in_=i)
        self.sems[sem] += 16
        val = self.sems[sem]
        step.incs.append((sem, 16))
        step._dma_val = val
        step.is_dma = True
        me = (sem, step, eng)
        wr = [self.alias.get(w if isinstance(w, str) else w.name,
                             w if isinstance(w, str) else w.name) for w in writes]
        rd = [self.alias.get(r if isinstance(r, str) else r.name,
                             r if isinstance(r, str) else r.name) for r in reads]
        for w in wr:
            self.last_w[w] = me
            self.readers[w] = []
        for r in rd:
            rl = self.readers.setdefault(r, [])
            if rl and rl[-1][0] == "eng" and rl[-1][1] is step:
                rl[-1] = me
        return (sem, val)

    def barrier(self):
        toks = {}
        for e in self.ENGINES:
            if self.last_step[e] is not None:
                toks[e] = self._engine_token(e, self.last_step[e])
        dma_toks = [(s, v) for s, v in self.sems.items()
                    if not s.startswith("tk_") and v > 0]
        for e in self.ENGINES:
            step = _Step(lambda eng: eng.nop(), self.seq, "barrier")
            self.seq += 1
            step.pos = len(self.steps[e])
            if e != "sync":
                step.drain = True
                self.last_drain[e] = step.seq
            wt = self.waited[e]
            for oe, (s, v) in toks.items():
                if oe == e:
                    continue
                if v > wt.get(s, 0):
                    step.waits.append((s, v))
                    wt[s] = v
            for (s, v) in dma_toks:
                if v > wt.get(s, 0):
                    step.waits.append((s, v))
                    wt[s] = v
            self.steps[e].append(step)
            self.last_step[e] = step

    def replay(self, block):
        def body(engname):
            def run(engine):
                for st in self.steps[engname]:
                    try:
                        for (s, v) in st.waits:
                            engine.wait_ge(self.sem_handles[s], v)
                        if st.drain and engname != "sync":
                            engine.drain()
                        inst = st.fn(engine)
                        if st.note:
                            inst.annotate(st.note)
                        for (s, a) in st.incs:
                            inst.then_inc(self.sem_handles[s], a)
                    except Exception:
                        import sys as _s
                        print(f"REPLAY FAIL eng={engname} note={st.note}", file=_s.stderr)
                        raise
            return run

        block.tensor(body("tensor"))
        block.vector(body("vector"))
        block.scalar(body("scalar"))
        block.gpsimd(body("gpsimd"))
        block.sync(body("sync"))



def build_program(S=2048, dbg=()):
    NT = S // 512
    NKC = S // 128
    NHALF = max(1, S // 1024)
    HTOK = S // NHALF

    nc = bass.Bass("TRN2", target_bir_lowering=False)
    es = contextlib.ExitStack()

    def din(name, shape, dt=F32):
        return nc.dram_tensor(name, shape, dt, kind="ExternalInput").ap()

    x_in = din("x_fm", [D, S])
    wq_g_in = din("g_wq_p", [D, D])
    kvd_in = din("g_kv_down", [D, LAT])
    kup_in = din("g_k_up_p", [LAT, LAT])
    vup_in = din("g_v_up", [LAT, LAT])
    wo_g_in = din("g_wo_p", [D, D])
    wq_l_in = din("l_wq_p", [D, D])
    wk_l_in = din("l_wk_p", [D, LAT])
    wv_l_in = din("l_wv", [D, LAT])
    wo_l_in = din("l_wo_p", [D, D])
    n1w_in = din("n1w", [128, NCH])
    n2w_in = din("n2w", [128, NCH])
    rw_in = din("router_w", [D, NE])
    rb_in = din("router_b", [1, NE])
    w1_in = din("w1", [NE, D, FF])
    b1_in = din("b1r", [128, NE * (FF // 128)])
    w2_in = din("w2", [NE, FF, D])
    b2_in = din("b2_all", [NE, D])
    cosg_in = din("cos_g", [128, S])
    sing_in = din("sin_g", [128, S])
    cosl_in = din("cos_l", [128, S])
    sinl_in = din("sin_l", [128, S])
    tril_in = din("tri_l", [128, 128])
    triu_in = din("tri_u", [128, 128])
    maskg_in = din("mask_g", [4, 128, 512], BF16)
    maskl_in = din("mask_l", [6, 128, 512], BF16)
    idf_in = din("ident_f32", [128, 128])
    idb_in = din("ident_bf16", [128, 128], BF16)
    ones128_in = din("ones_1x128", [1, 128])
    onescol_in = din("ones_c128", [128, 1])
    cpool_in = din("cpool", [128, 8])
    onesall_in = din("ones_all", [128, 128])
    pswap_in = din("p_swap", [128, 128])
    ones64_in = din("ones_1x64", [1, 64])
    blk2_in = din("blk2", [128, 2])
    blk2b_in = din("blk2b", [128, 2], BF16)
    ind2_in = din("ind2", [2, 128])
    ind2b_in = din("ind2b", [2, 128], BF16)

    out_t = nc.dram_tensor("out", [D, S], F32, kind="ExternalOutput").ap()

    y_sp = nc.dram_tensor("y_sp", [NT, NCH, 128, 512], F32, kind="Internal").ap()
    y2_sp = nc.dram_tensor("y2_sp", [NT, NCH, 128, 512], F32, kind="Internal").ap()
    q_sp = nc.dram_tensor("q_sp", [NT, NCH, 128, 512], BF16, kind="Internal").ap()
    ql_sp = nc.dram_tensor("ql_sp", [NT, NCH, 128, 512], BF16, kind="Internal").ap()
    z_sp = nc.dram_tensor("z_sp", [NT, NCH, 128, 512], BF16, kind="Internal").ap()
    xn_sp = nc.dram_tensor("xn_sp", [NT, NCH, 128, 512], F32, kind="Internal").ap()

    dbg_outs = {}
    def dbg_tensor(name, shape, dt=F32):
        ap = nc.dram_tensor("dbg_" + name, shape, dt, kind="ExternalOutput").ap()
        dbg_outs[name] = ap
        return ap

    em = EM(nc)

    def _sb(arena, name, shape, dt):
        t = arena.sbuf_tensor(name, shape, dt)
        em.alias[t.name] = name
        return t

    perm = nc.auto_arena()
    ones128 = _sb(perm, "ones128", [1, 128], F32)
    onescol = _sb(perm, "onescol", [128, 1], F32)
    cpool = _sb(perm, "cpool", [128, 8], F32)
    onesall = _sb(perm, "onesall", [128, 128], F32)
    pswap = _sb(perm, "pswap", [128, 128], F32)
    ones64 = _sb(perm, "ones64", [1, 64], F32)
    blk2 = _sb(perm, "blk2", [128, 2], F32)
    blk2b = _sb(perm, "blk2b", [128, 2], BF16)
    ind2 = _sb(perm, "ind2", [2, 128], F32)
    ind2b = _sb(perm, "ind2b", [2, 128], BF16)
    n1w = _sb(perm, "n1w", [128, NCH], F32)
    n2w = _sb(perm, "n2w", [128, NCH], F32)
    rwb = _sb(perm, "rwb", [128, NCH * NE], BF16)
    rbb = _sb(perm, "rbb", [1, NE], F32)
    b1b = _sb(perm, "b1b", [128, NE * (FF // 128)], F32)
    b2b = _sb(perm, "b2b", [16, D], F32)
    cfm = _sb(perm, "cfm", [16, S], F32)
    kg = _sb(perm, "kg", [128, 2 * S], BF16)
    vg = _sb(perm, "vg", [128, NKC * 260], BF16)
    kl = _sb(perm, "kl", [128, 2 * S], BF16)
    vl = _sb(perm, "vl", [128, NKC * 260], BF16)
    perm.allocate()

    PS = [es.enter_context(nc.psum_tensor(f"P{i}", [128, 512], F32)) for i in range(7)]
    PT = es.enter_context(nc.psum_tensor("PT", [128, 512], BF16))

    def mm(lhsT, rhs, out, start, stop, reads, bank, tile_position=None, note=""):
        def f(e, l=lhsT, r=rhs, o=out, s1=start, s2=stop, tp=tile_position):
            return e.matmul(o, l, r, start=s1, stop=s2, tile_position=tp)
        return em.op("tensor", f, reads=reads, writes=[bank], note=note)

    def v_op(fn, reads, writes, note=""):
        return em.op("vector", fn, reads=reads, writes=writes, note=note)

    def s_op(fn, reads, writes, note=""):
        return em.op("scalar", fn, reads=reads, writes=writes, note=note)

    def g_op(fn, reads, writes, note=""):
        return em.op("gpsimd", fn, reads=reads, writes=writes, note=note)

    def recip_any(dst, src, scratch, src_names, dst_name, scratch_name, tag=""):
        s_op(lambda e, d=scratch, s=src: e.activation(d, s, AF.Ln),
             reads=src_names, writes=[scratch_name], note="rl" + tag)
        s_op(lambda e, d=dst, s=scratch: e.activation(d, s, AF.Exp, scale=-1.0),
             reads=[scratch_name], writes=[dst_name], note="re" + tag)

    def rsqrt2(dst, src, src_names, dst_name, tag=""):
        s_op(lambda e, d=dst, s=src: e.activation(d, s, AF.Ln),
             reads=src_names, writes=[dst_name], note="ql" + tag)
        s_op(lambda e, d=dst: e.activation(d, d, AF.Exp, scale=-0.5),
             reads=[dst_name], writes=[dst_name], note="qe" + tag)

    def rsqrt_any(dst, src, scratch, src_names, dst_name, scratch_name, tag=""):
        s_op(lambda e, d=scratch, s=src: e.activation(d, s, AF.Ln),
             reads=src_names, writes=[scratch_name], note="ql" + tag)
        s_op(lambda e, d=dst, s=scratch: e.activation(d, s, AF.Exp, scale=-0.5),
             reads=[scratch_name], writes=[dst_name], note="qe" + tag)

    def recip_rows(dst, src, n, src_names, dst_name, newton=True):
        t1 = rows[0:1, 512 * 4: 512 * 4 + n]
        recip_any(dst, src, t1, src_names, dst_name, "rows", "r")

    def ld(dst_t, src_ap, sem, note, eng="sync"):
        return em.dma(eng, dst_t[:], src_ap, "s_" + note, reads=[],
                      writes=[dst_t.name], note=note)

    cp_tok = ld(cpool, cpool_in[:], "c_ld", "cpool")
    nc.const_aps.register(F32, 0.125, cpool[:, 4:5])
    nc.const_aps.register(F32, -1.0, cpool[:, 5:6])
    nc.const_aps.register(F32, -0.5, cpool[:, 6:7])
    for _e in EM.ENGINES:
        em.op(_e, lambda en: en.nop(), reads=[], writes=[], note="cpw",
              extra_waits=[cp_tok])
    ld(ones128, ones128_in[:], "c_ld", "ones128")
    ld(onescol, onescol_in[:], "c_ld", "onescol")
    ld(onesall, onesall_in[:], "c_ld", "onesall")
    ld(pswap, pswap_in[:], "c_ld", "pswap")
    ld(ones64, ones64_in[:], "c_ld", "ones64")
    ld(blk2, blk2_in[:], "c_ld", "blk2")
    ld(blk2b, blk2b_in[:], "c_ld", "blk2b")
    ld(ind2, ind2_in[:], "c_ld", "ind2")
    ld(ind2b, ind2b_in[:], "c_ld", "ind2b")
    ld(n1w, n1w_in[:], "c_ld", "n1w")
    ld(n2w, n2w_in[:], "c_ld", "n2w")
    em.dma("gpsimd", rwb[:].rearrange("p (c n) -> p c n", c=NCH),
           rw_in.rearrange("(c p) n -> p c n", p=128), "s_rw",
           reads=[], writes=["rwb"], note="rw")
    ld(rbb, rb_in[:], "c_ld", "rb")
    ld(b1b, b1_in[:], "c_ld", "b1")
    ld(b2b, b2_in[:], "c_ld", "b2")

    def wload(dst_t, src_ap, din_, dout, sem, cols=None, note="w"):
        nch = din_ // 128
        dst = dst_t[:] if cols is None else dst_t[:, cols[0]:cols[1]]
        return em.dma("gpsimd",
                      dst.rearrange("p (c n) -> p c n", c=nch),
                      src_ap.rearrange("(c p) n -> p c n", p=128),
                      sem if sem.startswith("wf") else "s_" + note,
                      reads=[], writes=[dst_t.name], note=note)

    pha = nc.auto_arena()
    wqg = _sb(pha, "wqg", [128, NCH * D], BF16)
    kvd = _sb(pha, "kvd", [128, NCH * LAT], BF16)
    kup = _sb(pha, "kup", [128, 2 * LAT], BF16)
    vup = _sb(pha, "vup", [128, 2 * LAT], BF16)
    cosg = _sb(pha, "cosg", [128, S], F32)
    sing = _sb(pha, "sing", [128, S], F32)
    rows = _sb(pha, "rowsA", [1, 2560], F32)
    rowsbP = [_sb(pha, f"rowsbA{i}", [2, 1536], F32) for i in range(2)]
    rowsb = rowsbP[0]
    idb = _sb(pha, "idbA", [128, 128], BF16)
    em.alias[rows.name] = "rows"
    em.alias[rowsbP[0].name] = "rowsb0"
    em.alias[rowsbP[1].name] = "rowsb1"
    em.alias[idb.name] = "idb"
    xta = [_sb(pha, f"xt{i}", [128, NCH * 512], F32) for i in range(2)]
    yta = _sb(pha, "yta", [128, NCH * 512], F32)
    t1a = _sb(pha, "t1a", [128, NCH * 512], BF16)
    sqa = [_sb(pha, f"sqa{i}", [128, 512], F32) for i in range(2)]
    pcav = [_sb(pha, f"pca{i}", [128, 512], F32) for i in range(2)]
    psav = [_sb(pha, f"psa{i}", [128, 512], F32) for i in range(2)]
    qsqv = [_sb(pha, f"qsq{i}", [128, 512], BF16) for i in range(2)]
    rbt2 = [_sb(pha, f"rbt2A{i}", [2, 1024], BF16) for i in range(2)]
    em.alias[rbt2[0].name] = "rowsb2_0"
    em.alias[rbt2[1].name] = "rowsb2_1"
    pca, psa = pcav[0], psav[0]
    qba = [_sb(pha, f"qba{i}", [128, 512], BF16) for i in range(2)]
    lata = _sb(pha, "lata", [128, 2 * 512], BF16)
    vtmp = _sb(pha, "vtmp", [128, 512], BF16)
    pha.allocate()

    wload(wqg, wq_g_in, D, D, "wA", note="wqg")
    wload(kvd, kvd_in, D, LAT, "wA", note="kvd")
    wload(kup, kup_in, LAT, LAT, "wA", note="kup")
    wload(vup, vup_in, LAT, LAT, "wA", note="vup")
    ld(cosg, cosg_in[:], "c_ld", "cosg")
    ld(sing, sing_in[:], "c_ld", "sing")
    ld(idb, idb_in[:], "c_ld", "idbA")

    v_op(lambda e: e.memset(vg[:], 1.0), reads=[], writes=["vg"], note="vg1")
    v_op(lambda e: e.memset(vl[:], 1.0), reads=[], writes=["vl"], note="vl1")

    def x_load(t, slot):
        return em.dma("sync",
                      xta[slot][:].rearrange("p (c n) -> p c n", c=NCH),
                      x_in.rearrange("(c p) n -> p c n", p=128)[:, :, 512 * t:512 * t + 512],
                      f"xld{slot}", reads=[], writes=[xta[slot].name], note=f"x{t}")

    x_load(0, 0)


    def row(i, off, n=512):
        return rows[0:1, off:off + n]

    def sum_over_chunks(src_tile, bank, square, tag, nch=NCH, width=512):
        sq_tok = []
        for c in range(nch):
            sl = src_tile[:, width * c: width * (c + 1)]
            if square:
                sb = sqa[c % 2]
                s_op(lambda e, o=sb[:, 0:width], i=sl: e.activation(o, i, AF.Square),
                     reads=[src_tile.name], writes=[sb.name], note=f"sq_{tag}{c}")
                srcap = sb[:, 0:width]
                rname = sb.name
            else:
                srcap = sl
                rname = src_tile.name
            mm(onescol[:], srcap, bank[0:1, :], start=(c == 0), stop=(c == nch - 1),
               reads=["onescol", rname], bank=bank, note=f"sum_{tag}{c}")

    def ln_mu(bank_s, roff, tag):
        mu = row(0, roff + 1024)
        v_op(lambda e, o=mu, i=bank_s[0:1, :]: e.tensor_scalar(o, i, 1.0 / D, None, AL.mult),
             reads=[bank_s.name], writes=["rows"], note=f"mu_{tag}")
        return mu

    def ln_rows(bank_q, eps, roff, tag):
        mu = row(0, roff + 1024)
        m2 = row(0, roff + 1536)
        v_op(lambda e, o=m2, i=mu: e.tensor_mul(o, i, i),
             reads=["rows"], writes=["rows"], note=f"m2_{tag}")
        q2 = row(0, roff + 2048)
        v_op(lambda e, o=q2, i=bank_q[0:1, :]: e.tensor_scalar(o, i, 1.0 / D, None, AL.mult),
             reads=[bank_q.name], writes=["rows"], note=f"q2_{tag}")
        var = row(0, roff + 2048)
        v_op(lambda e, o=var, a=q2, b=m2: e.tensor_sub(o, a, b),
             reads=["rows"], writes=["rows"], note=f"var_{tag}")
        v_op(lambda e, o=var, ep=eps: e.tensor_scalar(o, o, ep, None, AL.add),
             reads=["rows"], writes=["rows"], note=f"vep_{tag}")
        A = row(0, roff)
        rsqrt_any(A, var, row(0, roff + 1536), ["rows"], "rows", "rows", f"L{tag}")
        B = row(0, roff + 512)
        v_op(lambda e, o=B, a=mu, b=A: e.tensor_mul(o, a, b),
             reads=["rows"], writes=["rows"], note=f"nb_{tag}")
        v_op(lambda e, o=B: e.tensor_scalar(o, o, -1.0, None, AL.mult),
             reads=["rows"], writes=["rows"], note=f"nb2_{tag}")
        return A, B

    def rms_row(bank_q, eps, roff, tag):
        m = row(0, roff + 1024)
        v_op(lambda e, o=m, i=bank_q[0:1, :]:
             e.tensor_scalar(o, i, 1.0 / D, eps, AL.mult, AL.add),
             reads=[bank_q.name], writes=["rows"], note=f"rm_{tag}")
        R = row(0, roff)
        rsqrt_any(R, m, row(0, roff + 512), ["rows"], "rows", "rows", f"R{tag}")
        return R

    def bcast(src_row, bank, src_names, note, lhs=None, base=None):
        l = ones128[:] if lhs is None else lhs
        outap = bank[:, :] if base is None else bank[base[0]:base[1], :]
        tp = None if base is None else (0, base[0])
        mm(l, src_row, outap, True, True, reads=["ones128" if lhs is None else "x", *src_names],
           bank=bank, tile_position=tp, note=note)

    def rope_qknorm_start(bank, ctab, stab, ctname, tag, par):
        pc = pcav[par][:]
        ps_ = psav[par][:]
        pcn = "pca" + str(par)
        psn = "psa" + str(par)
        bank_s = PS[0 + par]
        s_op(lambda e, o=ps_, i=bank[:, :]: e.activation(o, i, AF.Identity),
             reads=[bank.name], writes=[psn], note=f"qc_{tag}")
        mm(pswap[:], ps_, bank_s[:, :], True, True,
           reads=["pswap", psn], bank=bank_s, note=f"sw_{tag}")
        v_op(lambda e, o=pc, p=ps_, c=ctab: e.tensor_mul(o, p, c),
             reads=[psn, ctname[0]], writes=[pcn], note=f"pc_{tag}")
        v_op(lambda e, o=ps_, p=bank_s[:, :], s=stab: e.tensor_mul(o, p, s),
             reads=[bank_s.name, ctname[1]], writes=[psn], note=f"ps_{tag}")
        v_op(lambda e, o=pc, b=ps_: e.tensor_add(o, o, b),
             reads=[pcn, psn], writes=[pcn], note=f"rc_{tag}")
        sb = qsqv[par]
        s_op(lambda e, o=sb[:, 0:512], i=pc: e.activation(o, i, AF.Square),
             reads=[pcn], writes=[sb.name], note=f"qn2_{tag}")
        return

    def rope_qknorm_finish(dest_ap, dest_name, tag, par):
        pc = pcav[par][:]
        pcn = "pca" + str(par)
        bank_m = PS[2 + par]
        rbt = rowsbP[par]
        rbn = "rowsb" + str(par)
        sb = qsqv[par]
        mm(blk2b[:], sb[:, 0:512], bank_m[0:2, :], True, True,
           reads=["blk2b", sb.name], bank=bank_m, note=f"qns_{tag}")
        qm = rbt[0:2, 0:512]
        v_op(lambda e, o=qm, i=bank_m[0:2, :]:
             e.tensor_scalar(o, i, 1.0 / HD, EPS_RMS, AL.mult, AL.add),
             reads=[bank_m.name], writes=[rbn], note=f"qnm_{tag}")
        rq = rbt[0:2, 512:1024]
        rsqrt2(rq, qm, [rbn], rbn, f"q{tag}")
        rqh = rbt2[par][0:2, 0:512]
        rql = rbt[0:2, 1024:1536]
        rqlb = rbt2[par][0:2, 512:1024]
        bn2 = "rowsb2_" + str(par)
        v_op(lambda e, o=rqh, i=rq: e.tensor_copy(o, i),
             reads=[rbn], writes=[bn2], note=f"qh_{tag}")
        v_op(lambda e, o=rql, a=rq, b=rqh: e.tensor_sub(o, a, b),
             reads=[rbn, bn2], writes=[rbn], note=f"qlo_{tag}")
        v_op(lambda e, o=rqlb, i=rql: e.tensor_copy(o, i),
             reads=[rbn], writes=[bn2], note=f"qlc_{tag}")
        mm(ind2b[:], rqh, bank_m[:, :], True, False,
           reads=["ind2b", bn2], bank=bank_m, note=f"qnb_{tag}")
        mm(ind2b[:], rqlb, bank_m[:, :], False, True,
           reads=["ind2b", bn2], bank=bank_m, note=f"qnb2_{tag}")
        v_op(lambda e, o=dest_ap, a=pc, b=bank_m[:, :]: e.tensor_mul(o, a, b),
             reads=[pcn, bank_m.name], writes=[dest_name], note=f"qnf_{tag}")

    for t in range(NT):
        xt = xta[t % 2]
        if t + 1 < NT:
            x_load(t + 1, (t + 1) % 2)
        sum_over_chunks(xt, PS[6], False, f"sx{t}")
        ln_mu(PS[6], 0, f"ln1_{t}")
        sum_over_chunks(xt, PS[6], True, f"sxx{t}")
        A, B = ln_rows(PS[6], EPS_LN, 0, f"ln1_{t}")
        bcast(A, PS[2], ["rows"], f"bA{t}")
        bcast(B, PS[3], ["rows"], f"bB{t}")
        for c in range(NCH):
            v_op(lambda e, o=yta[:, 512 * c:512 * (c + 1)], x=xt[:, 512 * c:512 * (c + 1)],
                 a=PS[2][:, :]: e.tensor_mul(o, x, a),
                 reads=[xt.name, "P2"], writes=["yta"], note=f"y1_{t}{c}")
            v_op(lambda e, o=yta[:, 512 * c:512 * (c + 1)], b=PS[3][:, :]: e.tensor_add(o, o, b),
                 reads=["P3", "yta"], writes=["yta"], note=f"y2_{t}{c}")
        em.dma("gpsimd", y_sp[t].rearrange("c p n -> p c n"),
               yta[:].rearrange("p (c n) -> p c n", c=NCH),
               f"ysp{t}", reads=["yta"], writes=["y_sp"], note=f"ysp{t}")
        sum_over_chunks(yta, PS[6], True, f"syy{t}")
        R = rms_row(PS[6], EPS_RMS, 0, f"rms1_{t}")
        bcast(R, PS[2], ["rows"], f"bR{t}")
        for c in range(NCH):
            v_op(lambda e, o=t1a[:, 512 * c:512 * (c + 1)], y=yta[:, 512 * c:512 * (c + 1)],
                 w=n1w[:, c:c + 1], r=PS[2][:, :]:
                 e.scalar_tensor_tensor(o, y, w, r, AL.mult, AL.mult),
                 reads=["yta", "n1w", "P2"], writes=["t1a"], note=f"t1_{t}{c}")
        pend = None
        for oc in range(NCH):
            bq = PS[4 + (oc % 2)]
            for c in range(NCH):
                mm(wqg[:, D * c + 128 * oc: D * c + 128 * (oc + 1)],
                   t1a[:, 512 * c:512 * (c + 1)], bq[:, :],
                   start=(c == 0), stop=(c == NCH - 1),
                   reads=["wqg", "t1a"], bank=bq, note=f"qp{t}_{oc}{c}")
            def _flush_q(p):
                qb_, tg_, par_, oc_ = p
                rope_qknorm_finish(qb_[:, 0:512], qb_.name, tg_, par_)
                em.dma("gpsimd", q_sp[t, oc_], qb_[:, 0:512], f"qsp{t}",
                       reads=[qb_.name], writes=["q_sp"], note=f"qsp{t}_{oc_}")
            if pend is not None:
                _flush_q(pend)
            rope_qknorm_start(bq, cosg[:, 512 * t:512 * (t + 1)],
                              sing[:, 512 * t:512 * (t + 1)], ("cosg", "sing"),
                              f"q{t}_{oc}", oc % 2)
            pend = (qba[oc % 2], f"q{t}_{oc}", oc % 2, oc)
        _flush_q(pend)
        for lc in range(2):
            bq = PS[4]
            for c in range(NCH):
                mm(kvd[:, LAT * c + 128 * lc: LAT * c + 128 * (lc + 1)],
                   t1a[:, 512 * c:512 * (c + 1)], bq[:, :],
                   start=(c == 0), stop=(c == NCH - 1),
                   reads=["kvd", "t1a"], bank=bq, note=f"lat{t}_{lc}{c}")
            s_op(lambda e, o=lata[:, 512 * lc:512 * (lc + 1)], i=bq[:, :]:
                 e.activation(o, i, AF.Identity),
                 reads=[bq.name], writes=["lata"], note=f"latc{t}_{lc}")
        kpend = None
        for kc2 in range(2):
            bq = PS[4 + (kc2 % 2)]
            for lc in range(2):
                mm(kup[:, LAT * lc + 128 * kc2: LAT * lc + 128 * (kc2 + 1)],
                   lata[:, 512 * lc:512 * (lc + 1)], bq[:, :],
                   start=(lc == 0), stop=(lc == 1),
                   reads=["kup", "lata"], bank=bq, note=f"kp{t}_{kc2}{lc}")
            if kpend is not None:
                rope_qknorm_finish(*kpend)
            rope_qknorm_start(bq, cosg[:, 512 * t:512 * (t + 1)],
                              sing[:, 512 * t:512 * (t + 1)], ("cosg", "sing"),
                              f"k{t}_{kc2}", kc2 % 2)
            kpend = (kg[:, S * kc2 + 512 * t: S * kc2 + 512 * (t + 1)], "kg",
                     f"k{t}_{kc2}", kc2 % 2)
        rope_qknorm_finish(*kpend)
        for vc in range(2):
            bq = PS[5]
            for lc in range(2):
                mm(vup[:, LAT * lc + 128 * vc: LAT * lc + 128 * (vc + 1)],
                   lata[:, 512 * lc:512 * (lc + 1)], bq[:, :],
                   start=(lc == 0), stop=(lc == 1),
                   reads=["vup", "lata"], bank=bq, note=f"vp{t}_{vc}{lc}")
            s_op(lambda e, o=vtmp[:, 0:512], i=bq[:, :]: e.activation(o, i, AF.Identity),
                 reads=[bq.name], writes=["vtmp"], note=f"vc{t}_{vc}")
            for sb in range(4):
                em.op("tensor",
                      lambda e, o=PT[:, 128 * sb:128 * (sb + 1)],
                      i=vtmp[:, 128 * sb:128 * (sb + 1)], idn=idb[:]: e.transpose(o, i, idn),
                      reads=["vtmp", "idb"], writes=["PT"], note=f"vt{t}_{vc}{sb}")
                kc = 4 * t + sb
                for half in range(2):
                    kh = 2 * vc + half
                    v_op(lambda e, o=vg[:, 260 * kc + 65 * kh: 260 * kc + 65 * kh + 64],
                         i=PT[:, 128 * sb + 64 * half: 128 * sb + 64 * half + 64]:
                         e.tensor_copy(o, i),
                         reads=["PT"], writes=["vg"], note=f"ve{t}_{vc}{sb}{half}")

    if "y" in dbg or "t1" in dbg or "q" in dbg or "kg" in dbg or "vg" in dbg:
        if "t1" in dbg:
            dap = dbg_tensor("t1", [128, NCH * 512], BF16)
            em.dma("sync", dap, t1a[:], "dbg", reads=["t1a"], writes=[], note="dbg_t1")
        if "kg" in dbg:
            dap = dbg_tensor("kg", [128, 2 * S], BF16)
            em.dma("sync", dap, kg[:], "dbg", reads=["kg"], writes=[], note="dbg_kg")
        if "vg" in dbg:
            dap = dbg_tensor("vg", [128, NKC * 260], BF16)
            em.dma("sync", dap, vg[:], "dbg", reads=["vg"], writes=[], note="dbg_vg")

    em.barrier()
    pha.free()

    def attention(t, qt_b, kbuf, vbuf, kb_name, vb_name, local, tag,
                  ot_tile, tmpb, pbuf):
        if local:
            kc_lo = max(0, 4 * t - 2)
        else:
            kc_lo = 0
        kc_hi = 4 * t + 4
        npend = None

        def _norm_fin(p):
            sA, sB, nA, nB, j_ = p
            mm(onesall[64:65, 0:64], sA[64:65, 0:512], PS[6][0:64, :], True, True,
               reads=["onesall", nA], bank=PS[6],
               tile_position=(64, 0), note=f"nbA{tag}{t}_{j_}")
            mm(onesall[64:65, 0:64], sB[64:65, 0:512], PS[3][0:64, :], True, True,
               reads=["onesall", nB], bank=PS[3],
               tile_position=(64, 0), note=f"nbB{tag}{t}_{j_}")
            v_op(lambda e, o=ot_tile[0:64, 512 * j_:512 * (j_ + 1)], a=sA[0:64, 0:512],
                 b=PS[6][0:64, :]: e.tensor_mul(o, a, b),
                 reads=[nA, "P6"], writes=[ot_tile.name], note=f"onA{tag}{t}_{j_}")
            v_op(lambda e, o=tmpb[0:64, 512 * j_:512 * (j_ + 1)], a=sB[0:64, 0:512],
                 b=PS[3][0:64, :]: e.tensor_mul(o, a, b),
                 reads=[nB, "P3"], writes=[tmpb.name], note=f"onB{tag}{t}_{j_}")

        for j in range(8):
            kvA = 0 if j < 4 else 2
            kvB = 1 if j < 4 else 3
            pending = []
            def flush_pending():
                for (pb_, kv_, kc_, par_) in pending:
                    bo = PS[4 + par_]
                    mm(vbuf[:, 260 * kc_ + 65 * kv_: 260 * kc_ + 65 * (kv_ + 1)],
                       pb_[:, 0:512], bo[0:65, 0:512],
                       start=(kc_ == kc_lo), stop=(kc_ == kc_hi - 1),
                       reads=[vb_name, pb_.name], bank=bo,
                       note=f"o{tag}{t}_{j}{kc_}{par_}")
                pending.clear()
            for kc in range(kc_lo, kc_hi):
                jj = kc - 4 * t
                if local:
                    c0, c1 = max(0, 128 * jj), min(512, 128 * jj + 384)
                else:
                    c0, c1 = max(0, 128 * jj), 512
                masked = local or jj >= 0
                for parity, kv in ((0, kvA), (1, kvB)):
                    bs = PS[parity * 2 + ((kc - kc_lo) % 2)]
                    base = 64 * parity
                    if masked:
                        mi = maskoff + jj
                        mm(idb[:], maskt[:, 512 * mi:512 * (mi + 1)], bs[:, :],
                           True, False, reads=["idb", maskname], bank=bs,
                           note=f"mk{tag}{t}_{j}{kc}{parity}")
                    mm(kbuf[base:base + 64, S * (kv // 2) + 128 * kc: S * (kv // 2) + 128 * (kc + 1)],
                       qt_b[base:base + 64, 512 * j + c0: 512 * j + c1],
                       bs[:, c0:c1], not masked, True,
                       reads=[kb_name, qt_b.name], bank=bs,
                       tile_position=(base, 0), note=f"s{tag}{t}_{j}{kc}{parity}")
                    pb = pbuf[parity * 4 + ((kc - kc_lo) % 4)]
                    s_op(lambda e, o=pb[:, :], i=bs[:, :]:
                         e.activation(o, i, AF.Exp, scale=0.125),
                         reads=[bs.name], writes=[pb.name], note=f"e{tag}{t}_{j}{kc}{parity}")
                    pending.append((pb, kv, kc, parity))
                if len(pending) >= 8:
                    flush_pending()
            flush_pending()
            if npend is not None:
                _norm_fin(npend)
            sA = sqa[0] if j % 2 == 0 else pcav[0]
            sB = sqa[1] if j % 2 == 0 else psav[0]
            scA = pcav[0] if j % 2 == 0 else sqa[0]
            scB = psav[0] if j % 2 == 0 else sqa[1]
            nA = em.alias.get(sA.name, sA.name)
            nB = em.alias.get(sB.name, sB.name)
            s_op(lambda e, o=sA[0:65, 0:512], i=PS[4][0:65, :]:
                 e.activation(o, i, AF.Identity),
                 reads=["P4"], writes=[nA], note=f"ocA{tag}{t}_{j}")
            s_op(lambda e, o=sB[0:65, 0:512], i=PS[5][0:65, :]:
                 e.activation(o, i, AF.Identity),
                 reads=["P5"], writes=[nB], note=f"ocB{tag}{t}_{j}")
            recip_any(sA[64:65, 0:512], sA[64:65, 0:512], scA[64:65, 0:512],
                      [nA], nA, em.alias.get(scA.name, scA.name), f"A{tag}{t}{j}")
            recip_any(sB[64:65, 0:512], sB[64:65, 0:512], scB[64:65, 0:512],
                      [nB], nB, em.alias.get(scB.name, scB.name), f"B{tag}{t}{j}")
            npend = (sA, sB, nA, nB, j)
        _norm_fin(npend)
        em.dma("sync", ot_tile[64:128, :], tmpb[0:64, :], f"sh{tag}{t}",
               reads=[tmpb.name], writes=[ot_tile.name], note=f"shift{tag}{t}")

    phb = nc.auto_arena()
    wog = _sb(phb, "wog", [128, NCH * D], BF16)
    wql = _sb(phb, "wql", [128, NCH * D], BF16)
    wkl = _sb(phb, "wkl", [128, NCH * LAT], BF16)
    wvl = _sb(phb, "wvl", [128, NCH * LAT], BF16)
    cosl = _sb(phb, "cosl", [128, S], F32)
    sinl = _sb(phb, "sinl", [128, S], F32)
    rowsB = _sb(phb, "rowsB", [1, 2560], F32)
    rowsbBP = [_sb(phb, f"rowsbB{i}", [2, 1536], F32) for i in range(2)]
    rowsbB = rowsbBP[0]
    idbB = _sb(phb, "idbB", [128, 128], BF16)
    maskgB = _sb(phb, "maskgB", [128, 4 * 512], BF16)
    qtb = [_sb(phb, "qtb0", [128, NCH * 512], BF16)]
    ytb = [_sb(phb, "ytb0", [128, NCH * 512], F32)]
    otb = _sb(phb, "otb", [128, NCH * 512], BF16)
    tmpb = _sb(phb, "tmpb", [64, NCH * 512], BF16)
    pbufb = [_sb(phb, f"pb{i}", [128, 512], BF16) for i in range(8)]
    t2b = _sb(phb, "t2b", [128, NCH * 512], BF16)
    sqb = [_sb(phb, f"sqb{i}", [128, 512], F32) for i in range(2)]
    pcbv = [_sb(phb, f"pcb{i}", [128, 512], F32) for i in range(2)]
    psbv = [_sb(phb, f"psb{i}", [128, 512], F32) for i in range(2)]
    qsqbv = [_sb(phb, f"qsqb{i}", [128, 512], BF16) for i in range(2)]
    rbt2B = [_sb(phb, f"rbt2B{i}", [2, 1024], BF16) for i in range(2)]
    em.alias[rbt2B[0].name] = "rowsb2_0"
    em.alias[rbt2B[1].name] = "rowsb2_1"
    pcb, psb = pcbv[0], psbv[0]
    qlb = [_sb(phb, f"qlb{i}", [128, 512], BF16) for i in range(2)]
    phb.allocate()

    sqa = sqb
    pca, psa, qba = pcb, psb, qlb
    pcav, psav, qsqv = pcbv, psbv, qsqbv
    rbt2 = rbt2B
    rows, rowsb, idb = rowsB, rowsbB, idbB
    rowsbP = rowsbBP
    em.alias[rowsB.name] = "rows"
    em.alias[rowsbBP[0].name] = "rowsb0"
    em.alias[rowsbBP[1].name] = "rowsb1"
    em.alias[idbB.name] = "idb"
    for _i in range(2):
        em.alias[pcbv[_i].name] = "pca" + str(_i)
        em.alias[psbv[_i].name] = "psa" + str(_i)
    em.alias[maskgB.name] = "maskg"
    maskt, maskname, maskoff = maskgB, "maskg", 0

    wload(wog, wo_g_in, D, D, "wB", note="wog")
    wload(wql, wq_l_in, D, D, "wB", note="wql")
    wload(wkl, wk_l_in, D, LAT, "wB", note="wkl")
    wload(wvl, wv_l_in, D, LAT, "wB", note="wvl")
    ld(cosl, cosl_in[:], "c_ld", "cosl")
    ld(sinl, sinl_in[:], "c_ld", "sinl")
    ld(idbB, idb_in[:], "c_ld", "idbB")
    em.dma("sync", maskgB[:].rearrange("p (c n) -> p c n", c=4),
           maskg_in.rearrange("c p n -> p c n"), "s_maskgB",
           reads=[], writes=["maskg"], note="maskgB")

    def q_load(t, slot, src, sem):
        return em.dma("sync", qtb[slot][:].rearrange("p (c n) -> p c n", c=NCH),
                      src[t].rearrange("c p n -> p c n"), sem,
                      reads=[src.tensor.name], writes=[qtb[slot].name], note=f"qld{t}")

    def y_load(t, slot, src, sem):
        return em.dma("sync", ytb[slot][:].rearrange("p (c n) -> p c n", c=NCH),
                      src[t].rearrange("c p n -> p c n"), sem,
                      reads=[src.tensor.name], writes=[ytb[slot].name], note=f"yld{t}")

    for t in range(NT):
        q_load(t, 0, q_sp, "qldB0")
        y_load(t, 0, y_sp, "yldB0")
        qt = qtb[0]
        yt = ytb[0]
        attention(t, qt, kg, vg, "kg", "vg", False, "G", otb, tmpb, pbufb)
        for oc in range(NCH):
            bg = PS[0 + (oc % 2)]
            for c in range(NCH):
                mm(wog[:, D * c + 128 * oc: D * c + 128 * (oc + 1)],
                   otb[:, 512 * c:512 * (c + 1)], bg[:, :],
                   start=(c == 0), stop=(c == NCH - 1),
                   reads=["wog", "otb"], bank=bg, note=f"wog{t}_{oc}{c}")
            v_op(lambda e, o=yt[:, 512 * oc:512 * (oc + 1)], g=bg[:, :]:
                 e.tensor_add(o, o, g),
                 reads=[bg.name, yt.name], writes=[yt.name], note=f"y2_{t}{oc}")
        em.dma("gpsimd", y2_sp[t].rearrange("c p n -> p c n"),
               yt[:].rearrange("p (c n) -> p c n", c=NCH),
               f"y2sp{t}", reads=[yt.name], writes=["y2_sp"], note=f"y2sp{t}")
        sum_over_chunks(yt, PS[6], True, f"sy2{t}")
        R = rms_row(PS[6], EPS_RMS, 0, f"rms2_{t}")
        bcast(R, PS[2], ["rows"], f"bR2{t}")
        for c in range(NCH):
            v_op(lambda e, o=t2b[:, 512 * c:512 * (c + 1)], y=yt[:, 512 * c:512 * (c + 1)],
                 w=n2w[:, c:c + 1], r=PS[2][:, :]:
                 e.scalar_tensor_tensor(o, y, w, r, AL.mult, AL.mult),
                 reads=[yt.name, "n2w", "P2"], writes=["t2b"], note=f"t2_{t}{c}")
        pend = None
        for oc in range(NCH):
            bq = PS[4 + (oc % 2)]
            for c in range(NCH):
                mm(wql[:, D * c + 128 * oc: D * c + 128 * (oc + 1)],
                   t2b[:, 512 * c:512 * (c + 1)], bq[:, :],
                   start=(c == 0), stop=(c == NCH - 1),
                   reads=["wql", "t2b"], bank=bq, note=f"qlp{t}_{oc}{c}")
            def _flush_ql(p):
                qb_, tg_, par_, oc_ = p
                rope_qknorm_finish(qb_[:, 0:512], qb_.name, tg_, par_)
                em.dma("gpsimd", ql_sp[t, oc_], qb_[:, 0:512], f"qlsp{t}",
                       reads=[qb_.name], writes=["ql_sp"], note=f"qlsp{t}_{oc_}")
            if pend is not None:
                _flush_ql(pend)
            rope_qknorm_start(bq, cosl[:, 512 * t:512 * (t + 1)],
                              sinl[:, 512 * t:512 * (t + 1)], ("cosl", "sinl"),
                              f"ql{t}_{oc}", oc % 2)
            pend = (qlb[oc % 2], f"ql{t}_{oc}", oc % 2, oc)
        _flush_ql(pend)
        klpend = None
        for kc2 in range(2):
            bq = PS[4 + (kc2 % 2)]
            for c in range(NCH):
                mm(wkl[:, LAT * c + 128 * kc2: LAT * c + 128 * (kc2 + 1)],
                   t2b[:, 512 * c:512 * (c + 1)], bq[:, :],
                   start=(c == 0), stop=(c == NCH - 1),
                   reads=["wkl", "t2b"], bank=bq, note=f"klp{t}_{kc2}{c}")
            if klpend is not None:
                rope_qknorm_finish(*klpend)
            rope_qknorm_start(bq, cosl[:, 512 * t:512 * (t + 1)],
                              sinl[:, 512 * t:512 * (t + 1)], ("cosl", "sinl"),
                              f"kl{t}_{kc2}", kc2 % 2)
            klpend = (kl[:, S * kc2 + 512 * t: S * kc2 + 512 * (t + 1)], "kl",
                      f"kl{t}_{kc2}", kc2 % 2)
        rope_qknorm_finish(*klpend)
        for vc in range(2):
            bq = PS[5]
            for c in range(NCH):
                mm(wvl[:, LAT * c + 128 * vc: LAT * c + 128 * (vc + 1)],
                   t2b[:, 512 * c:512 * (c + 1)], bq[:, :],
                   start=(c == 0), stop=(c == NCH - 1),
                   reads=["wvl", "t2b"], bank=bq, note=f"vlp{t}_{vc}{c}")
            s_op(lambda e, o=sqb[1][:, 0:512], i=bq[:, :]: e.activation(o, i, AF.Identity),
                 reads=[bq.name], writes=[sqb[1].name], note=f"vlc{t}_{vc}")
            v_op(lambda e, o=pbufb[3][:, 0:512], i=sqb[1][:, 0:512]: e.tensor_copy(o, i),
                 reads=[sqb[1].name], writes=[pbufb[3].name], note=f"vlcb{t}_{vc}")
            for sb in range(4):
                em.op("tensor",
                      lambda e, o=PT[:, 128 * sb:128 * (sb + 1)],
                      i=pbufb[3][:, 128 * sb:128 * (sb + 1)], idn=idb[:]: e.transpose(o, i, idn),
                      reads=[pbufb[3].name, "idb"], writes=["PT"], note=f"vlt{t}_{vc}{sb}")
                kc = 4 * t + sb
                for half in range(2):
                    kh = 2 * vc + half
                    v_op(lambda e, o=vl[:, 260 * kc + 65 * kh: 260 * kc + 65 * kh + 64],
                         i=PT[:, 128 * sb + 64 * half: 128 * sb + 64 * half + 64]:
                         e.tensor_copy(o, i),
                         reads=["PT"], writes=["vl"], note=f"vle{t}_{vc}{sb}{half}")

    em.barrier()
    phb.free()

    phc = nc.auto_arena()
    wol = _sb(phc, "wol", [128, NCH * D], BF16)
    rowsC = _sb(phc, "rowsC", [1, 2560], F32)
    rowsbCP = [_sb(phc, f"rowsbC{i}", [2, 64], F32) for i in range(2)]
    rowsbC = rowsbCP[0]
    idfC = _sb(phc, "idfC", [128, 128], F32)
    idbC = _sb(phc, "idbC", [128, 128], BF16)
    masklC = _sb(phc, "masklC", [128, 6 * 512], BF16)
    c_all = _sb(phc, "c_all", [128, (S // 128) * NE], F32)
    qtc = [_sb(phc, f"qtc{i}", [128, NCH * 512], BF16) for i in range(2)]
    xtc = [_sb(phc, f"xtc{i}", [128, NCH * 512], F32) for i in range(2)]
    ytc = [_sb(phc, f"ytc{i}", [128, NCH * 512], F32) for i in range(2)]
    otc = _sb(phc, "otc", [128, NCH * 512], BF16)
    tmpc = _sb(phc, "tmpc", [64, NCH * 512], BF16)
    pbufc = [_sb(phc, f"pc{i}", [128, 512], BF16) for i in range(8)]
    ztc = _sb(phc, "ztc", [128, NCH * 512], BF16)
    sqc = [_sb(phc, f"sqc{i}", [128, 512], F32) for i in range(2)]
    y1c = _sb(phc, "y1c", [128, 512], F32)
    pcc = _sb(phc, "pcc", [128, 512], F32)
    pscc = _sb(phc, "pscc", [128, 512], F32)
    lgc = _sb(phc, "lgc", [128, 16 * 4], F32)
    phc.allocate()

    sqa = sqc
    rows, rowsb, idf = rowsC, rowsbC, idfC
    rowsbP = rowsbCP
    pca, psa = pcc, pscc
    pcav = [pcc, pcc]
    psav = [pscc, pscc]
    em.alias[pcc.name] = "pca0"
    em.alias[pscc.name] = "psa0"
    em.alias[rowsC.name] = "rows"
    em.alias[rowsbCP[0].name] = "rowsb0"
    em.alias[rowsbCP[1].name] = "rowsb1"
    em.alias[idfC.name] = "idf"
    em.alias[masklC.name] = "maskl"
    em.alias[idbC.name] = "idb"
    idb = idbC
    maskt, maskname, maskoff = masklC, "maskl", 2

    wload(wol, wo_l_in, D, D, "wC", note="wol")
    ld(idfC, idf_in[:], "c_ld", "idfC")
    ld(idbC, idb_in[:], "c_ld", "idbC")
    em.dma("sync", masklC[:].rearrange("p (c n) -> p c n", c=6),
           maskl_in.rearrange("c p n -> p c n"), "s_masklC",
           reads=[], writes=["maskl"], note="masklC")

    def qc_load(t, slot):
        return em.dma("sync", qtc[slot][:].rearrange("p (c n) -> p c n", c=NCH),
                      ql_sp[t].rearrange("c p n -> p c n"), f"qldC{slot}",
                      reads=["ql_sp"], writes=[qtc[slot].name], note=f"qcl{t}")

    def xc_load(t, slot):
        return em.dma("sync", xtc[slot][:].rearrange("p (c n) -> p c n", c=NCH),
                      x_in.rearrange("(c p) n -> p c n", p=128)[:, :, 512 * t:512 * t + 512],
                      f"xldC{slot}", reads=[], writes=[xtc[slot].name], note=f"xcl{t}")

    def yc_load(t, slot):
        return em.dma("sync", ytc[slot][:].rearrange("p (c n) -> p c n", c=NCH),
                      y2_sp[t].rearrange("c p n -> p c n"), f"yldC{slot}",
                      reads=["y2_sp"], writes=[ytc[slot].name], note=f"ycl{t}")

    qc_load(0, 0)
    xc_load(0, 0)
    yc_load(0, 0)

    for t in range(NT):
        if t + 1 < NT:
            qc_load(t + 1, (t + 1) % 2)
            xc_load(t + 1, (t + 1) % 2)
            yc_load(t + 1, (t + 1) % 2)
        qt = qtc[t % 2]
        xt = xtc[t % 2]
        yt = ytc[t % 2]
        attention(t, qt, kl, vl, "kl", "vl", True, "L", otc, tmpc, pbufc)
        for c in range(NCH):
            v_op(lambda e, o=xt[:, 512 * c:512 * (c + 1)], y=yt[:, 512 * c:512 * (c + 1)]:
                 e.tensor_add(o, o, y),
                 reads=[yt.name, xt.name], writes=[xt.name], note=f"xy{t}{c}")
        for oc in range(NCH):
            bg = PS[0 + (oc % 2)]
            for c in range(NCH):
                mm(wol[:, D * c + 128 * oc: D * c + 128 * (oc + 1)],
                   otc[:, 512 * c:512 * (c + 1)], bg[:, :],
                   start=(c == 0), stop=(c == NCH - 1),
                   reads=["wol", "otc"], bank=bg, note=f"wol{t}_{oc}{c}")
            v_op(lambda e, o=xt[:, 512 * oc:512 * (oc + 1)], g=bg[:, :]:
                 e.tensor_add(o, o, g),
                 reads=[bg.name, xt.name], writes=[xt.name], note=f"xn_{t}{oc}")
        em.dma("gpsimd", xn_sp[t].rearrange("c p n -> p c n"),
               xt[:].rearrange("p (c n) -> p c n", c=NCH),
               f"xnsp{t}", reads=[xt.name], writes=["xn_sp"], note=f"xnsp{t}")
        sum_over_chunks(xt, PS[6], False, f"zx{t}")
        ln_mu(PS[6], 0, f"ln2_{t}")
        sum_over_chunks(xt, PS[6], True, f"zxx{t}")
        A, B = ln_rows(PS[6], EPS_LN, 0, f"ln2_{t}")
        bcast(A, PS[2], ["rows"], f"zA{t}")
        bcast(B, PS[3], ["rows"], f"zB{t}")
        for c in range(NCH):
            v_op(lambda e, o=y1c[:, 0:512], x=xt[:, 512 * c:512 * (c + 1)], a=PS[2][:, :]:
                 e.tensor_mul(o, x, a),
                 reads=[xt.name, "P2"], writes=["y1c"], note=f"z1_{t}{c}")
            v_op(lambda e, o=ztc[:, 512 * c:512 * (c + 1)], a=y1c[:, 0:512], b=PS[3][:, :]:
                 e.tensor_add(o, a, b),
                 reads=["y1c", "P3"], writes=["ztc"], note=f"z2_{t}{c}")
        em.dma("gpsimd", z_sp[t].rearrange("c p n -> p c n"),
               ztc[:].rearrange("p (c n) -> p c n", c=NCH),
               f"zsp{t}", reads=["ztc"], writes=["z_sp"], note=f"zsp{t}")
        for tc in range(4):
            bq = PS[4]
            for c in range(NCH):
                mm(ztc[:, 512 * c + 128 * tc: 512 * c + 128 * (tc + 1)],
                   rwb[:, NE * c: NE * (c + 1)], bq[0:128, 0:NE],
                   start=(c == 0), stop=False,
                   reads=["ztc", "rwb"], bank=bq, note=f"rt{t}{tc}{c}")
            mm(ones128[:], rbb[:], bq[0:128, 0:NE], start=False, stop=True,
               reads=["ones128", "rbb"], bank=bq, note=f"rtb{t}{tc}")
            lg = lgc[:, 16 * tc: 16 * tc + 16]
            v_op(lambda e, o=lg, i=bq[0:128, 0:NE]: e.tensor_copy(o, i),
                 reads=[bq.name], writes=["lgc"], note=f"lgc{t}{tc}")
            mx8 = sqc[0][:, 0:8]
            v_op(lambda e, o=mx8, i=lg: e.max(o, i),
                 reads=["lgc"], writes=[sqc[0].name], note=f"mx{t}{tc}")
            v_op(lambda e, o=sqc[0][:, 4:8]: e.memset(o, -2e30),
                 reads=[], writes=[sqc[0].name], note=f"mz{t}{tc}")
            wk = sqc[0][:, 8:24]
            v_op(lambda e, o=wk, m=sqc[0][:, 0:8], i=lg: e.match_replace(o, m, i, -1e30),
                 reads=[sqc[0].name, "lgc"], writes=[sqc[0].name], note=f"mr{t}{tc}")
            sel = sqc[0][:, 24:40]
            v_op(lambda e, o=sel, a=wk, b=lg: e.tensor_tensor(o, a, b, AL.not_equal),
                 reads=[sqc[0].name, "lgc"], writes=[sqc[0].name], note=f"sel{t}{tc}")
            nmx = sqc[0][:, 40:41]
            v_op(lambda e, o=nmx, i=sqc[0][:, 0:1]: e.tensor_scalar(o, i, -1.0, None, AL.mult),
                 reads=[sqc[0].name], writes=[sqc[0].name], note=f"nmx{t}{tc}")
            eg = sqc[0][:, 48:64]
            s_op(lambda e, o=eg, i=lg, b=nmx: e.activation(o, i, AF.Exp, bias=b),
                 reads=["lgc", sqc[0].name], writes=[sqc[0].name], note=f"eg{t}{tc}")
            es_ = sqc[0][:, 64:80]
            v_op(lambda e, o=es_, a=eg, b=sel: e.tensor_mul(o, a, b),
                 reads=[sqc[0].name], writes=[sqc[0].name], note=f"es{t}{tc}")
            ssum = sqc[0][:, 80:81]
            v_op(lambda e, o=ssum, i=es_: e.reduce_sum(o, i, axis=AX.X),
                 reads=[sqc[0].name], writes=[sqc[0].name], note=f"ss{t}{tc}")
            rs = sqc[0][:, 81:82]
            v_op(lambda e, o=rs, i=ssum: e.reciprocal(o, i),
                 reads=[sqc[0].name], writes=[sqc[0].name], note=f"rs{t}{tc}")
            gtc = 4 * t + tc
            v_op(lambda e, o=c_all[:, NE * gtc: NE * (gtc + 1)], a=es_, b=rs:
                 e.tensor_scalar(o, a, b, None, AL.mult),
                 reads=[sqc[0].name], writes=["c_all"], note=f"c{t}{tc}")
            em.op("tensor",
                  lambda e, o=PS[5][0:16, 0:128], i=c_all[:, NE * gtc: NE * (gtc + 1)],
                  idn=idf[:]: e.transpose(o, i, idn),
                  reads=["c_all", "idf"], writes=["P5"], note=f"ct{t}{tc}")
            v_op(lambda e, o=cfm[0:16, 128 * gtc:128 * (gtc + 1)], i=PS[5][0:16, 0:128]:
                 e.tensor_copy(o, i),
                 reads=["P5"], writes=["cfm"], note=f"ce{t}{tc}")

    em.barrier()
    phc.free()

    phf = nc.auto_arena()
    wbuf = [_sb(phf, f"wf{i}", [128, NCH * FF], BF16) for i in range(2)]
    NIC = FF // 128
    NTH = HTOK // 512
    hbuf = _sb(phf, "hbuf", [128, NIC * NTH * 512], BF16)
    zbuf = [_sb(phf, f"zf{i}", [128, NCH * 512], BF16) for i in range(NTH)]
    accf = _sb(phf, "accf", [128, NCH * HTOK], F32)
    cbc = [_sb(phf, f"cbc{i}", [128, 512], F32) for i in range(2)]
    tmpf = [_sb(phf, f"tmpf{i}", [128, 512], F32) for i in range(2)]
    crow2 = _sb(phf, "crow2", [1, 1024], F32)
    phf.allocate()

    def w1_load(e, slot):
        return wload(wbuf[slot], w1_in[e], D, FF, f"wf{slot}", note=f"w1_{e}")

    def w2_load(e, slot):
        return wload(wbuf[slot], w2_in[e], FF, D, f"wf{slot}", note=f"w2_{e}")

    def z_load(gt, slot):
        return em.dma("sync", zbuf[slot][:].rearrange("p (c n) -> p c n", c=NCH),
                      z_sp[gt].rearrange("c p n -> p c n"), f"zld{slot}",
                      reads=["z_sp"], writes=[zbuf[slot].name], note=f"zl{gt}")

    def acc_load(hf):
        t0 = hf * NTH
        for tt in range(NTH):
            em.dma("sync",
                   accf[:, :].rearrange("p (c h n) -> p c h n", c=NCH, h=NTH)[:, :, tt, :],
                   xn_sp[t0 + tt].rearrange("c p n -> p c n"),
                   "accld", reads=["xn_sp"], writes=["accf"], note=f"accl{hf}{tt}")

    for hf in range(NHALF):
        acc_load(hf)
        w1_load(0, 0)
        w2_load(0, 1)
        for e in range(NE):
            em.dma("sync", crow2[0:1, 0:HTOK],
                   cfm[e:e + 1, hf * HTOK:(hf + 1) * HTOK], f"crw{e % 2}",
                   reads=["cfm"], writes=["crow2"], note=f"crw{hf}{e}")
            for tt in range(NTH):
                gt = hf * NTH + tt
                if e == 0:
                    z_load(gt, tt)
                zb = zbuf[tt]
                for icc in range(NIC):
                    bh = PS[icc % 4]
                    for c in range(NCH):
                        mm(wbuf[0][:, FF * c + 128 * icc: FF * c + 128 * (icc + 1)],
                           zb[:, 512 * c:512 * (c + 1)], bh[:, :],
                           start=(c == 0), stop=(c == NCH - 1),
                           reads=[wbuf[0].name, zb.name], bank=bh,
                           note=f"m1_{hf}{e}{tt}{icc}{c}")
                    hcol = (icc * NTH + tt) * 512
                    s_op(lambda e_, o=hbuf[:, hcol:hcol + 512], i=bh[:, :],
                         b=b1b[:, NIC * e + icc: NIC * e + icc + 1]:
                         e_.activation(o, i, AF.Gelu, bias=b),
                         reads=[bh.name, "b1b"], writes=["hbuf"], note=f"g{hf}{e}{tt}{icc}")
            if e + 1 < NE:
                w1_load(e + 1, 0)
            for tt in range(NTH):
                gt512 = 512 * tt
                mm(ones128[:], crow2[0:1, gt512:gt512 + 512], PS[6][:, :],
                   True, True, reads=["ones128", "crow2"], bank=PS[6],
                   note=f"cb{hf}{e}{tt}")
                s_op(lambda e_, o=cbc[tt % 2][:, :], i=PS[6][:, :]:
                     e_.activation(o, i, AF.Identity),
                     reads=["P6"], writes=[cbc[tt % 2].name], note=f"cbe{hf}{e}{tt}")
                for fc in range(NCH):
                    bo = [PS[4], PS[5], PS[0], PS[1]][fc % 4]
                    for icc in range(NIC):
                        hcol = (icc * NTH + tt) * 512
                        mm(wbuf[1][:, D * icc + 128 * fc: D * icc + 128 * (fc + 1)],
                           hbuf[:, hcol:hcol + 512], bo[:, :],
                           start=(icc == 0), stop=(icc == NIC - 1),
                           reads=[wbuf[1].name, "hbuf"], bank=bo,
                           note=f"m2_{hf}{e}{tt}{fc}{icc}")
                    tf = tmpf[fc % 2]
                    v_op(lambda e_, o=tf[:, :], a=bo[:, :], b=cbc[tt % 2][:, :]:
                         e_.tensor_mul(o, a, b),
                         reads=[bo.name, cbc[tt % 2].name], writes=[tf.name],
                         note=f"fm{hf}{e}{tt}{fc}")
                    acc_sl = accf[:, HTOK * fc + 512 * tt: HTOK * fc + 512 * (tt + 1)]
                    v_op(lambda e_, o=acc_sl, a=tf[:, :]: e_.tensor_add(o, o, a),
                         reads=[tf.name, "accf"], writes=["accf"],
                         note=f"fa{hf}{e}{tt}{fc}")
            if e + 1 < NE:
                w2_load(e + 1, 1)
        for tt in range(NTH):
            gt512 = hf * HTOK + tt * 512
            for fc in range(NCH):
                bo = [PS[4], PS[5], PS[0], PS[1]][fc % 4]
                mm(b2b[:, 128 * fc:128 * (fc + 1)], cfm[0:16, gt512:gt512 + 512],
                   bo[:, :], True, True, reads=["b2b", "cfm"], bank=bo,
                   note=f"bc{hf}{tt}{fc}")
                acc_sl = accf[:, HTOK * fc + 512 * tt: HTOK * fc + 512 * (tt + 1)]
                v_op(lambda e_, o=acc_sl, a=bo[:, :]: e_.tensor_add(o, o, a),
                     reads=[bo.name, "accf"], writes=["accf"], note=f"ba{hf}{tt}{fc}")
        em.dma("gpsimd",
               out_t.rearrange("(c p) n -> p c n", p=128)[:, :, hf * HTOK:(hf + 1) * HTOK],
               accf[:].rearrange("p (c n) -> p c n", c=NCH),
               "outst", reads=["accf"], writes=["out"], note=f"out{hf}")
        if hf + 1 < NHALF:
            em.barrier()

    fin = em.op("sync", lambda e: e.nop(), reads=["out"], writes=[], note="fin")

    sem_names = sorted(em.sems.keys())
    with contextlib.ExitStack() as ses:
        for sn in sem_names:
            em.sem_handles[sn] = ses.enter_context(nc.semaphore(sn))
        with nc.Block() as block:
            em.replay(block)

    es.close()
    nc._em = em
    return nc, dbg_outs



def _prep_shared(inputs, S):
    pq = _perm_q()
    pk = _perm_k()
    po = _perm_o()
    f32 = np.float32
    d = {}
    d["g_wq_p"] = np.ascontiguousarray(inputs["L0_attn_g_wq"][:, pq], dtype=f32)
    d["g_kv_down"] = np.ascontiguousarray(inputs["L0_attn_g_kv_down"], dtype=f32)
    d["g_k_up_p"] = np.ascontiguousarray(inputs["L0_attn_g_k_up"][:, pk], dtype=f32)
    d["g_v_up"] = np.ascontiguousarray(inputs["L0_attn_g_v_up"], dtype=f32)
    d["g_wo_p"] = np.ascontiguousarray(inputs["L0_attn_g_wo"][po, :], dtype=f32)
    d["l_wq_p"] = np.ascontiguousarray(inputs["L0_attn_l_wq"][:, pq], dtype=f32)
    d["l_wk_p"] = np.ascontiguousarray(inputs["L0_attn_l_wk"][:, pk], dtype=f32)
    d["l_wv"] = np.ascontiguousarray(inputs["L0_attn_l_wv"], dtype=f32)
    d["l_wo_p"] = np.ascontiguousarray(inputs["L0_attn_l_wo"][po, :], dtype=f32)
    d["n1w"] = np.ascontiguousarray(
        np.asarray(inputs["L0_attn_norm1_w"], f32).reshape(NCH, 128).T)
    d["n2w"] = np.ascontiguousarray(
        np.asarray(inputs["L0_attn_norm2_w"], f32).reshape(NCH, 128).T)
    d["router_w"] = np.ascontiguousarray(inputs["L0_ffn_router_w"], dtype=f32)
    d["router_b"] = np.ascontiguousarray(
        np.asarray(inputs["L0_ffn_router_b"], f32).reshape(1, NE))
    d["w1"] = np.ascontiguousarray(inputs["L0_ffn_w1"], dtype=f32)
    d["b1r"] = np.ascontiguousarray(
        np.asarray(inputs["L0_ffn_b1"], f32).reshape(NE, FF // 128, 128)
        .transpose(2, 0, 1).reshape(128, NE * (FF // 128)))
    d["w2"] = np.ascontiguousarray(inputs["L0_ffn_w2"], dtype=f32)
    d["b2_all"] = np.ascontiguousarray(inputs["L0_ffn_b2"], dtype=f32)
    cg, sg = _rope_tables(S, BASE_G)
    cl, sl = _rope_tables(S, BASE_L)
    sign = np.where((np.arange(128) % 64) < 32, np.float32(-1.0), np.float32(1.0))
    d["cos_g"], d["sin_g"] = cg[:, :S], np.ascontiguousarray(sg[:, :S] * sign[:, None])
    d["cos_l"], d["sin_l"] = cl[:, :S], np.ascontiguousarray(sl[:, :S] * sign[:, None])
    psw = np.zeros((128, 128), np.float32)
    for r in range(128):
        b = (r // 64) * 64
        psw[r, b + (r - b + 32) % 64] = 1.0
    d["p_swap"] = psw
    tl, tu = _tri_masks()
    d["tri_l"], d["tri_u"] = tl, tu
    p_ = np.arange(128)[:, None]
    qf_ = np.arange(512)[None, :]
    mg = np.zeros((4, 128, 512), np.float32)
    for jj in range(4):
        mg[jj] = np.where(qf_ >= p_ + 128 * jj, 0.0, -1e30)
    mloc = np.zeros((6, 128, 512), np.float32)
    for ji, jj in enumerate(range(-2, 4)):
        dqk = qf_ - p_ - 128 * jj
        mloc[ji] = np.where((dqk >= 0) & (dqk < WIN), 0.0, -1e30)
    d["mask_g"] = mg.astype(ml_dtypes.bfloat16)
    d["mask_l"] = mloc.astype(ml_dtypes.bfloat16)
    d["ident_f32"] = np.eye(128, dtype=f32)
    d["ident_bf16"] = np.eye(128, dtype=ml_dtypes.bfloat16)
    d["ones_1x128"] = np.ones((1, 128), f32)
    d["ones_c128"] = np.ones((128, 1), f32)
    d["ones_all"] = np.ones((128, 128), f32)
    cp = np.zeros((128, 8), f32)
    cp[:, 0] = EPS_LN
    cp[:, 1] = EPS_RMS
    cp[:, 2] = 1.0 / D
    cp[:, 3] = 1.0 / HD
    cp[:, 4] = 0.125
    cp[:, 5] = -1.0
    cp[:, 6] = -0.5
    d["cpool"] = cp
    d["ones_1x64"] = np.ones((1, 64), f32)
    blk2 = np.zeros((128, 2), f32)
    blk2[0:64, 0] = 1.0
    blk2[64:128, 1] = 1.0
    d["blk2"] = blk2
    d["blk2b"] = blk2.astype(ml_dtypes.bfloat16)
    ind2 = np.zeros((2, 128), f32)
    ind2[0, 0:64] = 1.0
    ind2[1, 64:128] = 1.0
    d["ind2"] = ind2
    d["ind2b"] = ind2.astype(ml_dtypes.bfloat16)
    return d


def kernel(**inputs):
    hidden = np.asarray(inputs["L0_attn_hidden"], np.float32)
    B, S, _ = hidden.shape
    shared = _prep_shared(inputs, S)
    nc, _ = build_program(S=S)
    in_maps = []
    for b in range(B):
        m = dict(shared)
        m["x_fm"] = np.ascontiguousarray(hidden[b].T)
        in_maps.append(m)
    res = run_bass_kernel_spmd(nc, in_maps, core_ids=list(range(B)))
    out = np.stack([np.asarray(r["out"], np.float32).T for r in res.results])
    return np.ascontiguousarray(out)
